# version 05
# baseline (speedup 1.0000x reference)
import contextlib

import numpy as np
import ml_dtypes

import concourse.bass as bass
import concourse.mybir as mybir
from concourse.bass_utils import run_bass_kernel_spmd
from concourse.aot_env import allow_ici_all_to_all

F32 = mybir.dt.float32
BF16 = mybir.dt.bfloat16
AF = mybir.ActivationFunctionType
ALU = mybir.AluOpType
AX = mybir.AxisListType

D = 2048
NB = 2
HID = 8192
T = 8192
TOK = 2048
TT = 512
NT = TOK // TT
NCH = D // 128
NHT = HID // 128
G = 4
HPG = 4
DH = 128
QB = 512
NQB = T // QB
NJT = T // 128
LN_EPS = 1e-5
RMS_EPS = 1e-6
MULT = float(DH) ** -0.25
NEG_BIG = -1e30


class Sem:
    def __init__(self, h):
        self.h = h
        self.n = 0

    def inc(self, k):
        self.n += k
        return self.n


def build_nc(taps=()):
    nc = bass.Bass("TRN2", target_bir_lowering=False, debug=False)

    def din(name, shape, dt=F32):
        return nc.dram_tensor(name, shape, dt, kind="ExternalInput").ap()

    x_in = din("x", [D, TOK])
    wg_in = din("wg", [NHT, 128, 2048])
    wu_in = din("wu", [NHT, 128, 2048])
    wd_in = din("wd", [NCH, 128, 8192])
    wq_in = din("wq", [NCH, 128, 2048])
    wk_in = din("wk", [G, 128, 2048])
    wv_in = din("wv", [NCH, 128, 512])
    wo_in = din("wo", [NCH, 128, 2048])
    gq_in = din("gq", [128, 16])
    gk_in = din("gk", [128, 4])
    cos_in = din("cosw", [128, TOK])
    sin_in = din("sinw", [128, TOK])
    eyeb_in = din("eyeb", [128, 128], BF16)
    eyef_in = din("eyef", [128, 128], F32)
    onesb_in = din("onesb", [128, 1], BF16)
    onesf_in = din("onesf", [1, 128], F32)
    maskp_in = din("maskp", [128, 4 * 512], BF16)

    y_out = nc.dram_tensor("y", [D, TOK], F32, kind="ExternalOutput").ap()

    def dint(name, shape, dt):
        return nc.dram_tensor(name, shape, dt, kind="Internal")

    wg_b = dint("wg_b", [NHT, 128, 2048], BF16)
    wu_b = dint("wu_b", [NHT, 128, 2048], BF16)
    wd_b = dint("wd_b", [NCH, 128, 8192], BF16)
    wq_b = dint("wq_b", [NCH, 128, 2048], BF16)
    wk_b = dint("wk_b", [G, 128, 2048], BF16)
    wv_b = dint("wv_b", [NCH, 128, 512], BF16)
    wo_b = dint("wo_b", [NCH, 128, 2048], BF16)

    y1d = dint("y1d", [NCH, 128, TOK], F32)

    q_a2a_i = dint("q_a2a_i", [8, 512, TOK], BF16)
    q_a2a_o = dint("q_a2a_o", [8, 512, TOK], BF16)
    k_a2a_i = dint("k_a2a_i", [8, 128, TOK], BF16)
    k_a2a_o = dint("k_a2a_o", [8, 128, TOK], BF16)
    v_a2a_i = dint("v_a2a_i", [8, TOK, 128], BF16)
    v_a2a_o = dint("v_a2a_o", [8, TOK, 128], BF16)
    o_a2a_i = dint("o_a2a_i", [8, 512, TOK], BF16)
    o_a2a_o = dint("o_a2a_o", [8, 512, TOK], BF16)
    s_a2a_i = dint("s_a2a_i", [8, 4, TOK], F32)
    s_a2a_o = dint("s_a2a_o", [8, 4, TOK], F32)

    q_mine = dint("q_mine", [4, 512, TOK], BF16)
    o_mine = dint("o_mine", [NCH, 128, TOK], BF16)
    s_mine = dint("s_mine", [16, TOK], F32)
    s_scr = dint("s_scr", [4, NQB, QB], F32)
    y2d = dint("y2d", [NCH, 128, TOK], F32) if "y2" in taps else None
    obnd = dint("obnd", [NCH, 128, TOK], BF16) if "obn" in taps else None

    eyeb = nc.alloc_sbuf_tensor("eyeb_s", [128, 128], BF16)
    eyef = nc.alloc_sbuf_tensor("eyef_s", [128, 128], F32)
    onesb = nc.alloc_sbuf_tensor("onesb_s", [128, 1], BF16)
    onesf = nc.alloc_sbuf_tensor("onesf_s", [1, 128], F32)
    maskp = nc.alloc_sbuf_tensor("maskp_s", [128, 4 * 512], BF16)
    gq_raw = nc.alloc_sbuf_tensor("gq_raw", [128, 16], F32)
    gk_raw = nc.alloc_sbuf_tensor("gk_raw", [128, 4], F32)
    gqm = nc.alloc_sbuf_tensor("gqm_s", [128, 16], F32)
    gkm = nc.alloc_sbuf_tensor("gkm_s", [128, 4], F32)
    cost = nc.alloc_sbuf_tensor("cost", [128, TT], F32)
    sint = nc.alloc_sbuf_tensor("sint", [128, TT], F32)
    m_row = nc.alloc_sbuf_tensor("m_row", [1, TT], F32)
    v_row = nc.alloc_sbuf_tensor("v_row", [1, TT], F32)
    r_row = nc.alloc_sbuf_tensor("r_row", [1, TT], F32)
    n_row = nc.alloc_sbuf_tensor("n_row", [1, TT], F32)
    srow = nc.alloc_sbuf_tensor("srow", [1, 8], F32)
    sumrow = nc.alloc_sbuf_tensor("sumrow", [1, QB], F32)
    negM = nc.alloc_sbuf_tensor("negM", [128, 2], F32)
    maxcol = nc.alloc_sbuf_tensor("maxcol", [128, 1], F32)
    maxbuf = nc.alloc_sbuf_tensor("maxbuf", [128, NJT], F32)
    sqbf = nc.alloc_sbuf_tensor("sqbf", [128, TT], BF16)

    base = (nc.sbuf_base + 31) & ~31

    def sb_at(name, shape, dt, off):
        return nc.alloc_sbuf_tensor_at(name, shape, dt, offset=base + off)

    xa = sb_at("xa", [128, NCH * TT], F32, 0)
    xb = sb_at("xb", [128, NCH * TT], BF16, 32768)
    xln = sb_at("xln", [128, NCH * TT], BF16, 49152)
    gated = sb_at("gated", [128, NHT * TT], BF16, 65536)
    wsa = sb_at("wsa", [128, 2 * 2048], BF16, 131072)
    wsb = sb_at("wsb", [128, 2 * 2048], BF16, 139264)
    wsd = sb_at("wsd", [128, 2 * 8192], BF16, 147456)
    fsil = sb_at("fsil", [128, 512], F32, 180224)
    ropeA = sb_at("ropeA", [128, TT], F32, 182272)
    ropeB = sb_at("ropeB", [128, TT], F32, 184320)
    END_OFF = 186368
    assert base + END_OFF <= nc.sbuf_top, (base, END_OFF, nc.sbuf_top)

    qf = sb_at("qf", [128, NCH * TT], F32, 65536)
    kf = sb_at("kf", [128, G * TT], F32, 65536 + 32768)
    vtmp = sb_at("vtmp", [128, 4 * 512], BF16, 65536 + 40960)
    qkvb = sb_at("qkvb", [128, NCH * TT], BF16, 65536 + 45056)
    kb = sb_at("kb", [128, G * TT], BF16, 65536 + 61440)
    ob = sb_at("ob", [128, NCH * TT], BF16, 65536)
    sraw = sb_at("sraw", [1, 16 * TT], F32, 65536 + 16384)
    qn = sb_at("qn", [128, TT], F32, 180224)

    k_sb = sb_at("k_sb", [128, T], BF16, 0)
    v_sb = sb_at("v_sb", [128, NJT * 128], BF16, 16384)
    q_sbs = [sb_at("q_sb0", [128, T], BF16, 32768),
             sb_at("q_sb1", [128, T], BF16, 49152)]
    pts = [sb_at("pt0", [128, QB], BF16, 65536),
           sb_at("pt1", [128, QB], BF16, 66560)]
    obuf = sb_at("obuf", [128, QB], BF16, 67584)

    psA = [nc.alloc_psum_tensor(f"psA{i}", [128, 512], F32) for i in range(4)]
    psS = [nc.alloc_psum_tensor(f"psS{i}", [1, 512], F32) for i in range(2)]
    psB = [nc.alloc_psum_tensor(f"psB{i}", [128, 512], F32) for i in range(2)]

    stk = contextlib.ExitStack()
    sW = Sem(stk.enter_context(nc.semaphore("sW")))
    sL = Sem(stk.enter_context(nc.semaphore("sL")))
    sSt = Sem(stk.enter_context(nc.semaphore("sSt")))
    sP = Sem(stk.enter_context(nc.semaphore("sP")))
    sV = Sem(stk.enter_context(nc.semaphore("sV")))
    sA = Sem(stk.enter_context(nc.semaphore("sA")))
    sC = Sem(stk.enter_context(nc.semaphore("sC")))
    sO = Sem(stk.enter_context(nc.semaphore("sO")))

    plans = {k: [] for k in ["sync", "tensor", "vector", "scalar", "gpsimd"]}

    def mk(eng, build, waits=(), inc=None, amt=1):
        v = inc.inc(amt) if inc is not None else None
        w = tuple(x for x in waits if x is not None)

        def fn(e, build=build, w=w, inc=inc, amt=amt):
            for (s, val) in w:
                e.wait_ge(s.h, val)
            ins = build(e)
            if inc is not None:
                ins.then_inc(inc.h, amt)
        plans[eng].append(fn)
        return (inc, v) if inc is not None else None

    def dma(eng, out, in_, waits=(), sem=None):
        sem = sem if sem is not None else sL
        return mk(eng, lambda e, out=out, in_=in_: e.dma_start(out=out, in_=in_),
                  waits=waits, inc=sem, amt=16)

    def chunk(buf, cc, w=TT):
        return buf[:, cc * w:(cc + 1) * w]

    dma("sync", eyeb[:, :], eyeb_in)
    dma("sync", eyef[:, :], eyef_in)
    dma("sync", onesb[:, :], onesb_in)
    dma("sync", onesf[:, :], onesf_in)
    dma("sync", maskp[:, :], maskp_in)
    dma("sync", gq_raw[:, :], gq_in)
    dma("sync", gk_raw[:, :], gk_in)
    consts_ld = (sL, sL.n)
    mk("vector", lambda e: e.tensor_scalar_mul(gqm[:, :], gq_raw[:, :], MULT),
       waits=[consts_ld], inc=sV)
    mk("vector", lambda e: e.tensor_scalar_mul(gkm[:, :], gk_raw[:, :], MULT),
       waits=[consts_ld], inc=sV)
    gains_ready = (sV, sV.n)

    cast_marks = {}
    N0 = {"wg": NHT, "wu": NHT, "wd": NCH, "wq": NCH, "wk": G, "wv": NCH, "wo": NCH}

    def cast_weight(name, src, dst, nchunks):
        n0 = N0[name]
        step = n0 // nchunks
        marks = []
        for i in range(0, n0, step):
            dma("gpsimd", dst.ap()[i:i + step], src[i:i + step], sem=sW)
            marks.append((sW, sW.n))
        cast_marks[name] = marks

    cast_weight("wg", wg_in, wg_b, 8)
    cast_weight("wd", wd_in, wd_b, 4)
    cast_weight("wu", wu_in, wu_b, 8)
    cast_weight("wq", wq_in, wq_b, 2)
    cast_weight("wk", wk_in, wk_b, 1)
    cast_weight("wv", wv_in, wv_b, 1)
    cast_weight("wo", wo_in, wo_b, 2)

    def cast_ready(name, idx):
        marks = cast_marks[name]
        per = N0[name] // len(marks)
        return marks[min(idx // per, len(marks) - 1)]

    x_r = x_in.rearrange("(c p) t -> c p t", p=128)
    y_r = y_out.rearrange("(c p) t -> c p t", p=128)

    buf_last = {"xb": None, "wsa0": None, "wsa1": None, "wsb0": None,
                "wsb1": None, "wsd0": None, "wsd1": None}

    def layernorm(src_f32, dst_bf16, extra_wait=()):
        cast_done = mk("scalar", lambda e: e.activation(xb[:, :], src_f32[:, :], AF.Copy),
                       waits=list(extra_wait) + [buf_last["xb"]], inc=sA)
        for cc in range(NCH):
            mk("tensor",
               lambda e, cc=cc: e.matmul(psS[0][:, :], onesb[:, :], chunk(xb, cc),
                                         start=(cc == 0), stop=(cc == NCH - 1)),
               waits=[cast_done] if cc == 0 else (),
               inc=sP if cc == NCH - 1 else None)
        sum_done = (sP, sP.n)
        sq_done = mk("scalar", lambda e: e.activation(xb[:, :], xb[:, :], AF.Square),
                     waits=[sum_done], inc=sA)
        for cc in range(NCH):
            mk("tensor",
               lambda e, cc=cc: e.matmul(psS[1][:, :], onesb[:, :], chunk(xb, cc),
                                         start=(cc == 0), stop=(cc == NCH - 1)),
               waits=[sq_done] if cc == 0 else (),
               inc=sP if cc == NCH - 1 else None)
        sq_sum_done = (sP, sP.n)
        buf_last["xb"] = sq_sum_done
        mk("vector", lambda e: e.tensor_scalar_mul(m_row[:, :], psS[0][0:1, :], 1.0 / D),
           waits=[sum_done], inc=sV)
        mk("vector", lambda e: e.tensor_scalar_mul(v_row[:, :], psS[1][0:1, :], 1.0 / D),
           waits=[sq_sum_done], inc=sV)
        mk("scalar", lambda e: e.activation(r_row[:, :], m_row[:, :], AF.Square),
           waits=[(sV, sV.n)], inc=sA)
        mk("vector", lambda e: e.scalar_tensor_tensor(v_row[:, :], v_row[:, :], LN_EPS,
                                                      r_row[:, :], ALU.add, ALU.subtract),
           waits=[(sA, sA.n)], inc=sV)
        mk("scalar", lambda e: e.activation(r_row[:, :], v_row[:, :], AF.Sqrt),
           waits=[(sV, sV.n)], inc=sA)
        mk("vector", lambda e: e.reciprocal(r_row[:, :], r_row[:, :]),
           waits=[(sA, sA.n)], inc=sV)
        stats_done = mk("vector",
                        lambda e: e.scalar_tensor_tensor(n_row[:, :], m_row[:, :], -1.0,
                                                         r_row[:, :], ALU.mult, ALU.mult),
                        inc=sV)
        mk("tensor", lambda e: e.matmul(psB[0][:, :], onesf[:, :], r_row[:, :],
                                        start=True, stop=True),
           waits=[stats_done], inc=sP)
        bc_done = mk("tensor", lambda e: e.matmul(psB[1][:, :], onesf[:, :], n_row[:, :],
                                                  start=True, stop=True), inc=sP)
        marks = []
        for cc in range(NCH):
            mk("vector",
               lambda e, cc=cc: e.tensor_mul(ropeA[:, :], chunk(src_f32, cc), psB[0][:, :]),
               waits=[bc_done] if cc == 0 else ())
            marks.append(mk("vector",
                            lambda e, cc=cc: e.tensor_add(chunk(dst_bf16, cc), ropeA[:, :],
                                                          psB[1][:, :]),
                            inc=sV))
        return marks

    def ffn(src_bf16, dst_f32, scale, xln_marks):
        prev_gate_use = [None, None]
        prev_up_use = [None, None]
        for ht in range(NHT):
            par = ht % 2
            slab_g = wsa[:, par * 2048:(par + 1) * 2048]
            slab_u = wsb[:, par * 2048:(par + 1) * 2048]
            dma("sync", slab_g, wg_b.ap()[ht],
                waits=[cast_ready("wg", ht), buf_last[f"wsa{par}"]])
            g_ld = (sL, sL.n)
            dma("scalar", slab_u, wu_b.ap()[ht],
                waits=[cast_ready("wu", ht), buf_last[f"wsb{par}"]], sem=sSt)
            u_ld = (sSt, sSt.n)
            for cc in range(NCH):
                w = []
                if cc == 0:
                    w = [g_ld, prev_gate_use[par]]
                if ht == 0:
                    w.append(xln_marks[cc])
                mk("tensor",
                   lambda e, cc=cc, par=par, slab_g=slab_g: e.matmul(
                       psA[par][:, :], slab_g[:, cc * 128:(cc + 1) * 128],
                       chunk(src_bf16, cc), start=(cc == 0), stop=(cc == NCH - 1)),
                   waits=w, inc=sP if cc == NCH - 1 else None)
            gate_done = (sP, sP.n)
            buf_last[f"wsa{par}"] = gate_done
            for cc in range(NCH):
                mk("tensor",
                   lambda e, cc=cc, par=par, slab_u=slab_u: e.matmul(
                       psA[2 + par][:, :], slab_u[:, cc * 128:(cc + 1) * 128],
                       chunk(src_bf16, cc), start=(cc == 0), stop=(cc == NCH - 1)),
                   waits=[u_ld, prev_up_use[par]] if cc == 0 else (),
                   inc=sP if cc == NCH - 1 else None)
            up_done = (sP, sP.n)
            buf_last[f"wsb{par}"] = up_done
            sil_done = mk("scalar",
                          lambda e, par=par: e.activation(fsil[:, :], psA[par][:, :],
                                                          AF.Silu),
                          waits=[gate_done], inc=sA)
            prev_gate_use[par] = sil_done
            gmul = mk("vector",
                      lambda e, ht=ht, par=par: e.tensor_mul(
                          gated[:, ht * TT:(ht + 1) * TT], fsil[:, :], psA[2 + par][:, :]),
                      waits=[sil_done, up_done], inc=sV)
            prev_up_use[par] = gmul
        gated_done = (sV, sV.n)
        prev_down_use = list(prev_gate_use)
        down_marks = [None, None]
        res_done = None
        for dt in range(NCH):
            par = dt % 2
            dslab = wsd[:, par * 8192:(par + 1) * 8192]
            dma("sync", dslab, wd_b.ap()[dt],
                waits=[cast_ready("wd", dt), buf_last[f"wsd{par}"]])
            dslab_ld = (sL, sL.n)
            dwait = [dslab_ld, gated_done, prev_down_use[par]]
            for hc in range(NHT):
                mk("tensor",
                   lambda e, hc=hc, par=par, dslab=dslab: e.matmul(
                       psA[par][:, :], dslab[:, hc * 128:(hc + 1) * 128],
                       gated[:, hc * TT:(hc + 1) * TT], start=(hc == 0),
                       stop=(hc == NHT - 1)),
                   waits=dwait if hc == 0 else (),
                   inc=sP if hc == NHT - 1 else None)
            down_done = (sP, sP.n)
            down_marks[par] = down_done
            buf_last[f"wsd{par}"] = down_done
            res_done = mk("vector",
                          lambda e, dt=dt, par=par: e.scalar_tensor_tensor(
                              chunk(dst_f32, dt), psA[par][:, :], scale,
                              chunk(dst_f32, dt), ALU.mult, ALU.add),
                          waits=[down_done], inc=sV)
            prev_down_use[par] = res_done
        return res_done

    prev_tile_done = None
    for tt in range(NT):
        ts = slice(tt * TT, (tt + 1) * TT)
        dma("sync", xa.rearrange("p (c t) -> p c t", t=TT),
            x_in[:, ts].rearrange("(c p) t -> p c t", p=128),
            waits=[prev_tile_done, (sV, sV.n), (sA, sA.n)])
        x_ld = (sL, sL.n)
        dma("sync", cost[:, :], cos_in[:, ts], waits=[(sV, sV.n)])
        dma("sync", sint[:, :], sin_in[:, ts])
        trig_ld = (sL, sL.n)

        xln_marks = layernorm(xa, xln, extra_wait=[x_ld])
        y1_done = ffn(xln, xa, 0.5, xln_marks)
        dma("gpsimd", y1d.ap()[:, :, ts].rearrange("c p t -> p c t"),
            xa.rearrange("p (c t) -> p c t", t=TT),
            waits=[y1_done], sem=sSt)
        y1ln_marks = layernorm(xa, xln, extra_wait=[y1_done])
        y1ln_done = y1ln_marks[-1]

        bank_guard = [None, None, None, None]
        bank = 0

        def qkv_proj(n_out, slab_ld_for, slab_ap_for, evac_for, ready, buf_key_for=None):
            nonlocal bank
            for i in range(n_out):
                slab_ld = slab_ld_for(i)
                bwait = [slab_ld, ready, bank_guard[bank]]
                for cc in range(NCH):
                    mk("tensor",
                       lambda e, i=i, cc=cc, bank=bank, slab_ap_for=slab_ap_for:
                       e.matmul(psA[bank][:, :], slab_ap_for(i, cc), chunk(xln, cc),
                                start=(cc == 0), stop=(cc == NCH - 1)),
                       waits=bwait if cc == 0 else (),
                       inc=sP if cc == NCH - 1 else None)
                mmd = (sP, sP.n)
                if buf_key_for is not None:
                    buf_last[buf_key_for(i)] = mmd
                bank_guard[bank] = evac_for(i, bank, mmd)
                bank = (bank + 1) % 4

        def q_slab_ld(qt):
            par = qt % 2
            dma("sync", wsa[:, par * 2048:(par + 1) * 2048], wq_b.ap()[qt],
                waits=[cast_ready("wq", qt), buf_last[f"wsa{par}"]])
            return (sL, sL.n)

        def q_slab_ap(qt, cc):
            par = qt % 2
            return wsa[:, par * 2048 + cc * 128: par * 2048 + (cc + 1) * 128]

        def q_evac(qt, bk, mmd):
            return mk("scalar",
                      lambda e, qt=qt, bk=bk: e.activation(
                          chunk(qf, qt), psA[bk][:, :], AF.Copy),
                      waits=[mmd, gains_ready], inc=sA)

        qkv_proj(NCH, q_slab_ld, q_slab_ap, q_evac, y1ln_done,
                 buf_key_for=lambda qt: f"wsa{qt % 2}")

        dma("sync", wsd[:, 0:8192].rearrange("p (g t) -> p g t", t=2048),
            wk_b.ap().rearrange("g p t -> p g t"),
            waits=[cast_ready("wk", 0), buf_last["wsd0"]])
        wk_ld = (sL, sL.n)

        def k_slab_ld(kt):
            return wk_ld

        def k_slab_ap(kt, cc):
            return wsd[:, kt * 2048 + cc * 128: kt * 2048 + (cc + 1) * 128]

        def k_evac(kt, bk, mmd):
            return mk("scalar",
                      lambda e, kt=kt, bk=bk: e.activation(
                          chunk(kf, kt), psA[bk][:, :], AF.Copy),
                      waits=[mmd], inc=sA)

        qkv_proj(G, k_slab_ld, k_slab_ap, k_evac, y1ln_done,
                 buf_key_for=lambda kt: "wsd0")

        dma("sync", wsd[:, 8192:8192 + NCH * 512].rearrange("p (c v) -> p c v", v=512),
            wv_b.ap().rearrange("c p v -> p c v"),
            waits=[cast_ready("wv", 0), buf_last["wsd1"]])
        wv_ld = (sL, sL.n)
        for sub in range(4):
            bwait = [wv_ld, y1ln_done, bank_guard[bank]]
            for cc in range(NCH):
                mk("tensor",
                   lambda e, cc=cc, sub=sub, bank=bank: e.matmul(
                       psA[bank][:, :],
                       xln[:, cc * TT + sub * 128: cc * TT + (sub + 1) * 128],
                       wsd[:, 8192 + cc * 512: 8192 + (cc + 1) * 512],
                       start=(cc == 0), stop=(cc == NCH - 1)),
                   waits=bwait if cc == 0 else (),
                   inc=sP if cc == NCH - 1 else None)
            mmd = (sP, sP.n)
            buf_last["wsd1"] = mmd
            bank_guard[bank] = mk("scalar",
                                  lambda e, sub=sub, bank=bank: e.activation(
                                      vtmp[:, sub * 512:(sub + 1) * 512],
                                      psA[bank][:, :], AF.Copy),
                                  waits=[mmd], inc=sA)
            bank = (bank + 1) % 4
        v_done = (sA, sA.n)

        rr_state = {"prev_mm": None, "prev_bc_use": None}

        def rms_rope(src, dst, n_chunks, first_wait, gain):
            prev_mm = rr_state["prev_mm"]
            prev_bc_use = rr_state["prev_bc_use"]
            last = None
            for ci in range(n_chunks):
                sch = chunk(src, ci)
                dch = chunk(dst, ci)
                sqd = mk("scalar",
                         lambda e, sch=sch: e.activation(sqbf[:, :], sch, AF.Square),
                         waits=[first_wait, prev_mm] if ci == 0 else [prev_mm],
                         inc=sA)
                ssum = mk("tensor",
                          lambda e: e.matmul(psS[0][:, :], onesb[:, :], sqbf[:, :],
                                             start=True, stop=True),
                          waits=[sqd], inc=sP)
                prev_mm = ssum
                mk("vector",
                   lambda e: e.tensor_scalar(m_row[:, :], psS[0][0:1, :], 1.0 / DH,
                                             RMS_EPS, ALU.mult, ALU.add),
                   waits=[ssum], inc=sV)
                rms = mk("scalar", lambda e: e.activation(m_row[:, :], m_row[:, :],
                                                          AF.Sqrt),
                         waits=[(sV, sV.n)], inc=sA)
                rcp = mk("vector", lambda e: e.reciprocal(m_row[:, :], m_row[:, :]),
                         waits=[rms], inc=sV)
                bc = mk("tensor", lambda e: e.matmul(psB[0][:, :], onesf[:, :],
                                                     m_row[:, :], start=True, stop=True),
                        waits=[rcp, prev_bc_use], inc=sP)
                qd = mk("vector",
                        lambda e, sch=sch: e.tensor_mul(qn[:, :], sch, psB[0][:, :]),
                        waits=[bc], inc=sV)
                prev_bc_use = qd
                mk("vector",
                   lambda e, ci=ci: e.tensor_scalar_mul(qn[:, :], qn[:, :],
                                                        gain[:, ci:ci + 1]))
                mk("vector", lambda e: e.tensor_copy(ropeA[0:64, :], qn[64:128, :]),
                   waits=[trig_ld])
                mk("vector", lambda e: e.tensor_copy(ropeA[64:128, :], qn[0:64, :]))
                mk("vector", lambda e: e.tensor_mul(ropeB[:, :], qn[:, :], cost[:, :]))
                mk("vector", lambda e: e.tensor_mul(ropeA[:, :], ropeA[:, :], sint[:, :]))
                last = mk("vector",
                          lambda e, dch=dch: e.tensor_add(dch, ropeB[:, :], ropeA[:, :]),
                          inc=sV)
            rr_state["prev_mm"] = prev_mm
            rr_state["prev_bc_use"] = prev_bc_use
            return last

        q_rope_done = rms_rope(qf, qkvb, NCH, (sA, sA.n), gqm)
        k_rope_done = rms_rope(kf, kb, G, (sA, sA.n), gkm)

        qkvb_r = qkvb.rearrange("p (c t) -> p c t", t=TT)
        for d in range(8):
            g2 = d % 4
            dma("gpsimd",
                q_a2a_i.ap()[d].rearrange("(c p) t -> p c t", p=128)[:, :, ts],
                qkvb_r[:, 4 * g2:4 * g2 + 4, :],
                waits=[q_rope_done], sem=sSt)
            dma("gpsimd", k_a2a_i.ap()[d][:, ts], kb[:, g2 * TT:(g2 + 1) * TT],
                waits=[k_rope_done], sem=sSt)
            dma("gpsimd",
                v_a2a_i.ap()[d][tt * TT:(tt + 1) * TT, :]
                .rearrange("(s p) v -> p s v", p=128),
                vtmp.rearrange("p (s v) -> p s v", v=512)[:, :, g2 * 128:(g2 + 1) * 128],
                waits=[v_done], sem=sSt)
        prev_tile_done = (sSt, sSt.n)

    groups8 = [list(range(8))]

    def coll(ins_ap, outs_ap, waits):
        def fn(e, ins_ap=ins_ap, outs_ap=outs_ap,
               waits=tuple(x for x in waits if x is not None)):
            for (s, val) in waits:
                e.wait_ge(s.h, val)
            with allow_ici_all_to_all():
                e.collective_compute("AllToAll", ALU.bypass, replica_groups=groups8,
                                     ins=[ins_ap], outs=[outs_ap]).then_inc(sC.h, 1)
        sC.inc(1)
        plans["gpsimd"].append(fn)
        return (sC, sC.n)

    a2a1_wait = [(sSt, sSt.n)]
    coll(q_a2a_i.ap().rearrange("s q t -> (s q) t"),
         q_a2a_o.ap().rearrange("s q t -> (s q) t"), a2a1_wait)
    coll(k_a2a_i.ap().rearrange("s q t -> (s q) t"),
         k_a2a_o.ap().rearrange("s q t -> (s q) t"), [])
    a2a1_done = coll(v_a2a_i.ap().rearrange("s q t -> (s q) t"),
                     v_a2a_o.ap().rearrange("s q t -> (s q) t"), [])

    def switch_copy(pairs_fn, waits, n_dmas, sem):
        v = sem.inc(16 * n_dmas)

        def fn(e, waits=tuple(x for x in waits if x is not None)):
            for (s, val) in waits:
                e.wait_ge(s.h, val)
            pid = e.partition_id()
            for i in nc.Switch(engines=[e], index=[pid], n=8):
                prs = pairs_fn(i)
                assert len(prs) == n_dmas
                for (o, inp) in prs:
                    e.dma_start(out=o, in_=inp).then_inc(sem.h, 16)
        plans["sync"].append(fn)
        return (sem, v)

    def qkv_pairs(core):
        b = core // 4
        out = []
        for r in range(4):
            out.append((q_mine.ap()[r], q_a2a_o.ap()[4 * b + r]))
            out.append((k_sb[:, r * TOK:(r + 1) * TOK], k_a2a_o.ap()[4 * b + r]))
            out.append((v_sb[:, r * TOK:(r + 1) * TOK].rearrange("p (j d) -> p j d", d=128),
                        v_a2a_o.ap()[4 * b + r].rearrange("(j p) d -> p j d", p=128)))
        return out

    kv_done = switch_copy(qkv_pairs, [a2a1_done], 12, sL)
    qmine_done = kv_done

    qh_marks = {}
    for h in range(2):
        dma("sync", q_sbs[h].rearrange("p (r t) -> p r t", r=4),
            q_mine.ap()[:, h * 128:(h + 1) * 128, :].rearrange("r p t -> p r t"),
            waits=[qmine_done])
        qh_marks[h] = (sL, sL.n)

    p1_guard = [None, None]
    p2_guard = [None, None]
    prev_o_evac = None
    prev_sum_cp = None
    prev_mrow = None
    prev_negM_rd = {}
    prev_obuf_store = None
    negM_marks = {}
    att_tail = []

    def emit_pass2(h, qb):
        nonlocal prev_o_evac, prev_sum_cp, prev_obuf_store
        nj = 4 * qb + 4
        qs = slice(qb * QB, (qb + 1) * QB)
        qsb = q_sbs[h % 2]
        negM_ap = negM[:, qb % 2:qb % 2 + 1]
        pt_rd = {}
        first_exp = None
        last_exp = None
        for jt in range(nj):
            par = jt % 2
            diag_r = jt - 4 * qb
            w0 = [p2_guard[par]]
            mk("tensor",
               lambda e, jt=jt, par=par, qs=qs, qsb=qsb, diag=(diag_r >= 0):
               e.matmul(psA[2 + par][:, :], k_sb[:, jt * 128:(jt + 1) * 128],
                        qsb[:, qs], start=True, stop=(not diag)),
               waits=w0,
               inc=None if diag_r >= 0 else sP)
            if diag_r >= 0:
                mk("tensor",
                   lambda e, par=par, r=diag_r: e.matmul(
                       psA[2 + par][:, :], eyeb[:, :],
                       maskp[:, r * 512:(r + 1) * 512], start=False, stop=True),
                   inc=sP)
            st2 = (sP, sP.n)
            ewaits = [st2, pt_rd.get(jt - 2)]
            if jt == 0:
                ewaits.append(negM_marks[(h, qb)])
            ex = mk("scalar",
                    lambda e, jt=jt, par=par, negM_ap=negM_ap: e.activation(
                        pts[jt % 2][:, :], psA[2 + par][:, :], AF.Exp, bias=negM_ap),
                    waits=ewaits, inc=sA)
            p2_guard[par] = ex
            last_exp = ex
            if first_exp is None:
                first_exp = ex
            pvw = [ex]
            if jt == 0:
                pvw.append(prev_o_evac)
            mk("tensor",
               lambda e, jt=jt: e.matmul(psB[0][:, :], v_sb[:, jt * 128:(jt + 1) * 128],
                                         pts[jt % 2][:, :], start=(jt == 0),
                                         stop=(jt == nj - 1)),
               waits=pvw, inc=sP if jt == nj - 1 else None)
            sw = []
            if jt == 0:
                sw.append(prev_sum_cp)
            mk("tensor",
               lambda e, jt=jt: e.matmul(psS[0][:, :], onesb[:, :], pts[jt % 2][:, :],
                                         start=(jt == 0), stop=(jt == nj - 1)),
               waits=sw, inc=sP if jt == nj - 1 else None)
            pt_rd[jt] = (sP, sP.n)
        prev_negM_rd[(h, qb)] = last_exp
        pv_done = (sP, sP.n)
        o_evac = mk("scalar",
                    lambda e: e.activation(obuf[:, :], psB[0][:, :], AF.Copy),
                    waits=[pv_done, prev_obuf_store], inc=sA)
        prev_o_evac = o_evac
        sum_cp = mk("vector", lambda e: e.tensor_copy(sumrow[:, :], psS[0][0:1, :]),
                    waits=[pv_done], inc=sV)
        prev_sum_cp = sum_cp
        quarter = qb // 4
        cslice = slice((qb % 4) * 512, (qb % 4) * 512 + 512)
        for d in (quarter, quarter + 4):
            dma("gpsimd", o_a2a_i.ap()[d][h * 128:(h + 1) * 128, cslice], obuf[:, :],
                waits=[o_evac], sem=sSt)
        prev_obuf_store = (sSt, sSt.n)
        dma("gpsimd", s_scr.ap()[h][qb:qb + 1, :], sumrow[:, :],
            waits=[sum_cp], sem=sSt)

    for h in range(HPG):
        q_ld = qh_marks[h]
        qsb = q_sbs[h % 2]
        for qb in range(NQB):
            nj = 4 * qb + 4
            qs = slice(qb * QB, (qb + 1) * QB)
            for jt in range(nj):
                par = jt % 2
                diag_r = jt - 4 * qb
                w0 = [q_ld, kv_done, p1_guard[par]]
                mk("tensor",
                   lambda e, jt=jt, par=par, qs=qs, qsb=qsb, diag=(diag_r >= 0):
                   e.matmul(psA[par][:, :], k_sb[:, jt * 128:(jt + 1) * 128],
                            qsb[:, qs], start=True, stop=(not diag)),
                   waits=w0, inc=None if diag_r >= 0 else sP)
                if diag_r >= 0:
                    mk("tensor",
                       lambda e, par=par, r=diag_r: e.matmul(
                           psA[par][:, :], eyeb[:, :],
                           maskp[:, r * 512:(r + 1) * 512], start=False, stop=True),
                       inc=sP)
                st_done = (sP, sP.n)
                p1_guard[par] = mk("vector",
                                   lambda e, jt=jt, par=par: e.reduce_max(
                                       maxbuf[:, jt:jt + 1], psA[par][:, :], axis=AX.X),
                                   waits=[st_done], inc=sV)
            if qb > 0:
                emit_pass2(h, qb - 1)
            mcol = mk("vector",
                      lambda e, nj=nj: e.reduce_max(maxcol[:, :], maxbuf[:, 0:nj],
                                                    axis=AX.X),
                      inc=sV)
            tr = mk("tensor",
                    lambda e: e.transpose(psS[1][0:1, 0:128], maxcol[:, :], eyef[:, :]),
                    waits=[mcol, prev_mrow], inc=sP)
            mrow = mk("vector",
                      lambda e: e.reduce_max(srow[0:1, 0:1], psS[1][0:1, 0:128],
                                             axis=AX.X, negate=True),
                      waits=[tr], inc=sV)
            prev_mrow = mrow
            bcM = mk("tensor",
                     lambda e: e.matmul(psB[1][:, 0:1], onesf[:, :], srow[0:1, 0:1],
                                        start=True, stop=True),
                     waits=[mrow, negM_marks.get((h, qb - 1))], inc=sP)
            negM_marks[(h, qb)] = mk("vector",
                                     lambda e, qb=qb: e.tensor_copy(
                                         negM[:, qb % 2:qb % 2 + 1], psB[1][:, 0:1]),
                                     waits=[bcM, prev_negM_rd.get((h, qb - 2))],
                                     inc=sV)
        emit_pass2(h, NQB - 1)
        if h + 2 < HPG:
            dma("sync", q_sbs[h % 2].rearrange("p (r t) -> p r t", r=4),
                q_mine.ap()[:, (h + 2) * 128:(h + 3) * 128, :].rearrange("r p t -> p r t"),
                waits=[(sP, sP.n)])
            qh_marks[h + 2] = (sL, sL.n)

    att_stores = (sSt, sSt.n)
    for d in range(8):
        g2 = d % 4
        dma("gpsimd", s_a2a_i.ap()[d],
            s_scr.ap().rearrange("h q b -> h (q b)")[:, g2 * TOK:(g2 + 1) * TOK],
            waits=[att_stores], sem=sSt)

    coll(o_a2a_i.ap().rearrange("s q t -> (s q) t"),
         o_a2a_o.ap().rearrange("s q t -> (s q) t"), [(sSt, sSt.n)])
    a2a2_done = coll(s_a2a_i.ap().rearrange("s q t -> (s q) t"),
                     s_a2a_o.ap().rearrange("s q t -> (s q) t"), [])

    def o_pairs(core):
        b = core // 4
        out = []
        om = o_mine.ap().rearrange("(g x) p t -> g (x p) t", g=4)
        sm = s_mine.ap().rearrange("(g h) t -> g h t", g=4)
        for g2 in range(4):
            out.append((om[g2], o_a2a_o.ap()[4 * b + g2]))
            out.append((sm[g2], s_a2a_o.ap()[4 * b + g2]))
        return out

    omine_done = switch_copy(o_pairs, [a2a2_done], 8, sL)

    prev_tile_done = None
    for tt in range(NT):
        ts = slice(tt * TT, (tt + 1) * TT)
        dma("sync", ob.rearrange("p (c t) -> p c t", t=TT),
            o_mine.ap()[:, :, ts].rearrange("c p t -> p c t"),
            waits=[omine_done, prev_tile_done, (sP, sP.n), (sV, sV.n)])
        dma("sync", sraw[0:1, :].rearrange("a (c t) -> a c t", c=16),
            s_mine.ap().rearrange("(a c) t -> a c t", a=1)[:, :, ts])
        dma("sync", xa.rearrange("p (c t) -> p c t", t=TT),
            y1d.ap()[:, :, ts].rearrange("c p t -> p c t"))
        c_ld = (sL, sL.n)
        rcp = mk("vector", lambda e: e.reciprocal(sraw[:, :], sraw[:, :]),
                 waits=[c_ld], inc=sV)
        prev_norm = [None, None]
        norm_done = None
        for cc in range(NCH):
            par = cc % 2
            bc = mk("tensor",
                    lambda e, cc=cc, par=par: e.matmul(
                        psB[par][:, :], onesf[:, :], sraw[0:1, cc * TT:(cc + 1) * TT],
                        start=True, stop=True),
                    waits=[rcp, prev_norm[par]], inc=sP)
            norm_done = mk("vector",
                           lambda e, cc=cc, par=par: e.tensor_mul(
                               chunk(ob, cc), chunk(ob, cc), psB[par][:, :]),
                           waits=[bc], inc=sV)
            prev_norm[par] = norm_done
        prev_res = [None, None]
        y2_done = None
        for dt in range(NCH):
            par = dt % 2
            dma("sync", wsa[:, par * 2048:(par + 1) * 2048], wo_b.ap()[dt],
                waits=[cast_ready("wo", dt), buf_last[f"wsa{par}"]])
            slab_ld = (sL, sL.n)
            mwait = [slab_ld, norm_done, prev_res[par]]
            for cc in range(NCH):
                mk("tensor",
                   lambda e, dt=dt, cc=cc, par=par: e.matmul(
                       psA[par][:, :],
                       wsa[:, par * 2048 + cc * 128: par * 2048 + (cc + 1) * 128],
                       chunk(ob, cc), start=(cc == 0), stop=(cc == NCH - 1)),
                   waits=mwait if cc == 0 else (),
                   inc=sP if cc == NCH - 1 else None)
            mmd = (sP, sP.n)
            buf_last[f"wsa{par}"] = mmd
            y2_done = mk("vector",
                         lambda e, dt=dt, par=par: e.tensor_add(
                             chunk(xa, dt), psA[par][:, :], chunk(xa, dt)),
                         waits=[mmd], inc=sV)
            prev_res[par] = y2_done
        if y2d is not None:
            dma("gpsimd", y2d.ap()[:, :, ts].rearrange("c p t -> p c t"),
                xa.rearrange("p (c t) -> p c t", t=TT), waits=[y2_done], sem=sSt)
        if obnd is not None:
            dma("gpsimd", obnd.ap()[:, :, ts].rearrange("c p t -> p c t"),
                ob.rearrange("p (c t) -> p c t", t=TT), waits=[norm_done], sem=sSt)
        y2ln_marks = layernorm(xa, xln, extra_wait=[y2_done])
        y3_done = ffn(xln, xa, 0.5, y2ln_marks)
        dma("gpsimd", y_r[:, :, ts].rearrange("c p t -> p c t"),
            xa.rearrange("p (c t) -> p c t", t=TT),
            waits=[y3_done], sem=sO)
        prev_tile_done = (sO, sO.n)

    tap_map = {
        "y1": (y1d, [NCH, 128, TOK], F32),
        "y2": (y2d, [NCH, 128, TOK], F32),
        "obn": (obnd, [NCH, 128, TOK], BF16),
        "qin": (q_a2a_i, [8, 512, TOK], BF16),
        "kin": (k_a2a_i, [8, 128, TOK], BF16),
        "vin": (v_a2a_i, [8, TOK, 128], BF16),
        "omine": (o_mine, [NCH, 128, TOK], BF16),
        "smine": (s_mine, [16, TOK], F32),
    }
    for tname in taps:
        src, shape, dt = tap_map[tname]
        tout = nc.dram_tensor("t_" + tname, shape, dt, kind="ExternalOutput").ap()
        dma("gpsimd", tout, src.ap(), waits=[(sO, sO.n), (sSt, sSt.n)], sem=sO)

    fin = (sO, sO.n)

    def fin_fn(e, fin=fin):
        e.wait_ge(fin[0].h, fin[1])
    plans["gpsimd"].append(fin_fn)

    with nc.Block() as block:
        @block.sync
        def _(e):
            for fn in plans["sync"]:
                fn(e)

        @block.tensor
        def _(e):
            for fn in plans["tensor"]:
                fn(e)

        @block.vector
        def _(e):
            for fn in plans["vector"]:
                fn(e)

        @block.scalar
        def _(e):
            for fn in plans["scalar"]:
                fn(e)

        @block.gpsimd
        def _(e):
            for fn in plans["gpsimd"]:
                fn(e)

    stk.close()
    return nc


def _host_inputs(x, wq, wk, wv, wo, gq, gk, w_gate, w_up, w_down):
    bf = ml_dtypes.bfloat16
    wg_sw = np.ascontiguousarray(
        w_gate.reshape(NHT, 128, NCH, 128).transpose(0, 3, 2, 1)).reshape(NHT, 128, 2048)
    wu_sw = np.ascontiguousarray(
        w_up.reshape(NHT, 128, NCH, 128).transpose(0, 3, 2, 1)).reshape(NHT, 128, 2048)
    wd_sw = np.ascontiguousarray(
        w_down.reshape(NCH, 128, NHT, 128).transpose(0, 3, 2, 1)).reshape(NCH, 128, 8192)
    wq_sw = np.ascontiguousarray(
        wq.reshape(NCH, 128, NCH, 128).transpose(2, 1, 0, 3)).reshape(NCH, 128, 2048)
    wk_sw = np.ascontiguousarray(
        wk.reshape(NCH, 128, G, 128).transpose(2, 1, 0, 3)).reshape(G, 128, 2048)
    wv_n = np.ascontiguousarray(wv.reshape(NCH, 128, 512))
    wo_sw = np.ascontiguousarray(
        wo.reshape(NCH, 128, NCH, 128).transpose(2, 1, 0, 3)).reshape(NCH, 128, 2048)
    gq_t = np.ascontiguousarray(gq.reshape(16, 128).T)
    gk_t = np.ascontiguousarray(gk.T)

    dims = np.arange(64, dtype=np.float64)
    ang = np.power(10000.0, -dims / 64.0)
    pos = np.arange(T, dtype=np.float64)
    rad = (pos[None, :] * ang[:, None]).astype(np.float32)
    c64 = np.cos(rad).astype(np.float32)
    s64 = np.sin(rad).astype(np.float32)
    cosw = np.concatenate([c64, c64], axis=0)
    sinw = np.concatenate([-s64, s64], axis=0)

    eyeb = np.eye(128, dtype=bf)
    eyef = np.eye(128, dtype=np.float32)
    onesb = np.ones((128, 1), dtype=bf)
    onesf = np.ones((1, 128), dtype=np.float32)
    maskp = np.zeros((128, 4 * 512), dtype=np.float32)
    jj = np.arange(128)[:, None]
    qq = np.arange(512)[None, :]
    for r in range(4):
        maskp[:, r * 512:(r + 1) * 512] = np.where(qq < jj + 128 * r, NEG_BIG, 0.0)
    maskp = maskp.astype(bf)

    shared = dict(wg=wg_sw, wu=wu_sw, wd=wd_sw, wq=wq_sw, wk=wk_sw, wv=wv_n,
                  wo=wo_sw, gq=gq_t, gk=gk_t, eyeb=eyeb, eyef=eyef,
                  onesb=onesb, onesf=onesf, maskp=maskp)

    in_maps = []
    for c in range(8):
        b, q4 = c // 4, c % 4
        t0 = q4 * TOK
        im = dict(shared)
        im["x"] = np.ascontiguousarray(x[b, t0:t0 + TOK, :].T)
        im["cosw"] = np.ascontiguousarray(cosw[:, t0:t0 + TOK])
        im["sinw"] = np.ascontiguousarray(sinw[:, t0:t0 + TOK])
        in_maps.append(im)
    return in_maps


def kernel(L0_attn_x, L0_attn_wq, L0_attn_wk, L0_attn_wv, L0_attn_wo,
           L0_attn_gq, L0_attn_gk, L0_ffn_w_gate, L0_ffn_w_up, L0_ffn_w_down):
    x = np.asarray(L0_attn_x, dtype=np.float32)
    in_maps = _host_inputs(x,
                           np.asarray(L0_attn_wq, np.float32),
                           np.asarray(L0_attn_wk, np.float32),
                           np.asarray(L0_attn_wv, np.float32),
                           np.asarray(L0_attn_wo, np.float32),
                           np.asarray(L0_attn_gq, np.float32),
                           np.asarray(L0_attn_gk, np.float32),
                           np.asarray(L0_ffn_w_gate, np.float32),
                           np.asarray(L0_ffn_w_up, np.float32),
                           np.asarray(L0_ffn_w_down, np.float32))
    nc = build_nc()
    res = run_bass_kernel_spmd(nc, in_maps, core_ids=list(range(8)))
    out = np.empty((NB, T, D), dtype=np.float32)
    for c in range(8):
        b, q4 = c // 4, c % 4
        t0 = q4 * TOK
        out[b, t0:t0 + TOK, :] = res.results[c]["y"].T
    return out


# version 07
# speedup vs baseline: 1.0361x; 1.0361x over previous
import contextlib

import numpy as np
import ml_dtypes

import concourse.bass as bass
import concourse.mybir as mybir
from concourse.bass_utils import run_bass_kernel_spmd
from concourse.aot_env import allow_ici_all_to_all

F32 = mybir.dt.float32
BF16 = mybir.dt.bfloat16
AF = mybir.ActivationFunctionType
ALU = mybir.AluOpType
AX = mybir.AxisListType

D = 2048
NB = 2
HID = 8192
T = 8192
TOK = 2048
TT = 512
NT = TOK // TT
NCH = D // 128
NHT = HID // 128
G = 4
HPG = 4
DH = 128
QB = 512
NQB = T // QB
NJT = T // 128
LN_EPS = 1e-5
RMS_EPS = 1e-6
MULT = float(DH) ** -0.25
NEG_BIG = -1e30


class Sem:
    def __init__(self, h):
        self.h = h
        self.n = 0

    def inc(self, k):
        self.n += k
        return self.n


def build_nc(taps=()):
    nc = bass.Bass("TRN2", target_bir_lowering=False, debug=False)

    def din(name, shape, dt=F32):
        return nc.dram_tensor(name, shape, dt, kind="ExternalInput").ap()

    x_in = din("x", [D, TOK])
    wg_in = din("wg", [NHT, 128, 2048])
    wu_in = din("wu", [NHT, 128, 2048])
    wd_in = din("wd", [NCH, 128, 8192])
    wq_in = din("wq", [NCH, 128, 2048])
    wk_in = din("wk", [G, 128, 2048])
    wv_in = din("wv", [NCH, 128, 512])
    wo_in = din("wo", [NCH, 128, 2048])
    gq_in = din("gq", [128, 16])
    gk_in = din("gk", [128, 4])
    cos_in = din("cosw", [128, TOK])
    sin_in = din("sinw", [128, TOK])
    eyeb_in = din("eyeb", [128, 128], BF16)
    eyef_in = din("eyef", [128, 128], F32)
    onesb_in = din("onesb", [128, 1], BF16)
    onesf_in = din("onesf", [1, 128], F32)
    maskp_in = din("maskp", [128, 4 * 512], BF16)

    y_out = nc.dram_tensor("y", [D, TOK], F32, kind="ExternalOutput").ap()

    def dint(name, shape, dt):
        return nc.dram_tensor(name, shape, dt, kind="Internal")

    wg_b = dint("wg_b", [NHT, 128, 2048], BF16)
    wu_b = dint("wu_b", [NHT, 128, 2048], BF16)
    wd_b = dint("wd_b", [NCH, 128, 8192], BF16)
    wq_b = dint("wq_b", [NCH, 128, 2048], BF16)
    wk_b = dint("wk_b", [G, 128, 2048], BF16)
    wv_b = dint("wv_b", [NCH, 128, 512], BF16)
    wo_b = dint("wo_b", [NCH, 128, 2048], BF16)

    y1d = dint("y1d", [NCH, 128, TOK], F32)

    q_a2a_i = dint("q_a2a_i", [8, 512, TOK], BF16)
    q_a2a_o = dint("q_a2a_o", [8, 512, TOK], BF16)
    k_a2a_i = dint("k_a2a_i", [8, 128, TOK], BF16)
    k_a2a_o = dint("k_a2a_o", [8, 128, TOK], BF16)
    v_a2a_i = dint("v_a2a_i", [8, TOK, 128], BF16)
    v_a2a_o = dint("v_a2a_o", [8, TOK, 128], BF16)
    o_a2a_i = dint("o_a2a_i", [8, 512, TOK], BF16)
    o_a2a_o = dint("o_a2a_o", [8, 512, TOK], BF16)
    s_a2a_i = dint("s_a2a_i", [8, 4, TOK], F32)
    s_a2a_o = dint("s_a2a_o", [8, 4, TOK], F32)

    q_mine = dint("q_mine", [4, 512, TOK], BF16)
    o_mine = dint("o_mine", [NCH, 128, TOK], BF16)
    s_mine = dint("s_mine", [16, TOK], F32)
    s_scr = dint("s_scr", [4, NQB, QB], F32)
    y2d = dint("y2d", [NCH, 128, TOK], F32) if "y2" in taps else None
    obnd = dint("obnd", [NCH, 128, TOK], BF16) if "obn" in taps else None

    eyeb = nc.alloc_sbuf_tensor("eyeb_s", [128, 128], BF16)
    eyef = nc.alloc_sbuf_tensor("eyef_s", [128, 128], F32)
    onesb = nc.alloc_sbuf_tensor("onesb_s", [128, 1], BF16)
    onesf = nc.alloc_sbuf_tensor("onesf_s", [1, 128], F32)
    maskp = nc.alloc_sbuf_tensor("maskp_s", [128, 4 * 512], BF16)
    gq_raw = nc.alloc_sbuf_tensor("gq_raw", [128, 16], F32)
    gk_raw = nc.alloc_sbuf_tensor("gk_raw", [128, 4], F32)
    gqm = nc.alloc_sbuf_tensor("gqm_s", [128, 16], F32)
    gkm = nc.alloc_sbuf_tensor("gkm_s", [128, 4], F32)
    cost = nc.alloc_sbuf_tensor("cost", [128, TT], F32)
    sint = nc.alloc_sbuf_tensor("sint", [128, TT], F32)
    m_row = nc.alloc_sbuf_tensor("m_row", [1, TT], F32)
    v_row = nc.alloc_sbuf_tensor("v_row", [1, TT], F32)
    r_row = nc.alloc_sbuf_tensor("r_row", [1, TT], F32)
    n_row = nc.alloc_sbuf_tensor("n_row", [1, TT], F32)
    srow = nc.alloc_sbuf_tensor("srow", [1, 8], F32)
    sumrows = [nc.alloc_sbuf_tensor("sumrow0", [1, QB], F32),
               nc.alloc_sbuf_tensor("sumrow1", [1, QB], F32)]
    negM = nc.alloc_sbuf_tensor("negM", [128, 2], F32)
    maxcol = nc.alloc_sbuf_tensor("maxcol", [128, 1], F32)
    maxbuf = nc.alloc_sbuf_tensor("maxbuf", [128, NJT], F32)
    sqbf = nc.alloc_sbuf_tensor("sqbf", [128, TT], BF16)

    base = (nc.sbuf_base + 31) & ~31

    def sb_at(name, shape, dt, off):
        return nc.alloc_sbuf_tensor_at(name, shape, dt, offset=base + off)

    xa = sb_at("xa", [128, NCH * TT], F32, 0)
    xb = sb_at("xb", [128, NCH * TT], BF16, 32768)
    xln = sb_at("xln", [128, NCH * TT], BF16, 49152)
    gated = sb_at("gated", [128, NHT * TT], BF16, 65536)
    wsa = sb_at("wsa", [128, 2 * 2048], BF16, 131072)
    wsb = sb_at("wsb", [128, 2 * 2048], BF16, 139264)
    wsd = sb_at("wsd", [128, 2 * 8192], BF16, 147456)
    fsil = sb_at("fsil", [128, 512], F32, 180224)
    ropeA = sb_at("ropeA", [128, TT], F32, 182272)
    ropeB = sb_at("ropeB", [128, TT], F32, 184320)
    END_OFF = 186368
    assert base + END_OFF <= nc.sbuf_top, (base, END_OFF, nc.sbuf_top)

    qf = sb_at("qf", [128, NCH * TT], F32, 65536)
    kf = sb_at("kf", [128, G * TT], F32, 65536 + 32768)
    vtmp = sb_at("vtmp", [128, 4 * 512], BF16, 65536 + 40960)
    qkvb = sb_at("qkvb", [128, NCH * TT], BF16, 65536 + 45056)
    kb = sb_at("kb", [128, G * TT], BF16, 65536 + 61440)
    ob = sb_at("ob", [128, NCH * TT], BF16, 65536)
    sraw = sb_at("sraw", [1, 16 * TT], F32, 65536 + 16384)
    qn = sb_at("qn", [128, TT], F32, 180224)

    k_sb = sb_at("k_sb", [128, T], BF16, 0)
    v_sb = sb_at("v_sb", [128, NJT * 128], BF16, 16384)
    q_sbs = [sb_at("q_sb0", [128, T], BF16, 32768),
             sb_at("q_sb1", [128, T], BF16, 49152)]
    pts = [sb_at("pt0", [128, QB], BF16, 65536),
           sb_at("pt1", [128, QB], BF16, 66560),
           sb_at("pt2", [128, QB], BF16, 67584),
           sb_at("pt3", [128, QB], BF16, 68608)]
    obufs = [sb_at("obuf0", [128, QB], BF16, 69632),
             sb_at("obuf1", [128, QB], BF16, 70656)]

    psA = [nc.alloc_psum_tensor(f"psA{i}", [128, 512], F32) for i in range(5)]
    psS = [nc.alloc_psum_tensor("psS0", [1, 512], F32)]
    psB = [nc.alloc_psum_tensor(f"psB{i}", [128, 512], F32) for i in range(2)]

    stk = contextlib.ExitStack()
    sW = Sem(stk.enter_context(nc.semaphore("sW")))
    sL = Sem(stk.enter_context(nc.semaphore("sL")))
    sSt = Sem(stk.enter_context(nc.semaphore("sSt")))
    sP = Sem(stk.enter_context(nc.semaphore("sP")))
    sV = Sem(stk.enter_context(nc.semaphore("sV")))
    sA = Sem(stk.enter_context(nc.semaphore("sA")))
    sC = Sem(stk.enter_context(nc.semaphore("sC")))
    sO = Sem(stk.enter_context(nc.semaphore("sO")))

    plans = {k: [] for k in ["sync", "tensor", "vector", "scalar", "gpsimd"]}

    def mk(eng, build, waits=(), inc=None, amt=1):
        v = inc.inc(amt) if inc is not None else None
        w = tuple(x for x in waits if x is not None)

        def fn(e, build=build, w=w, inc=inc, amt=amt):
            for (s, val) in w:
                e.wait_ge(s.h, val)
            ins = build(e)
            if inc is not None:
                ins.then_inc(inc.h, amt)
        plans[eng].append(fn)
        return (inc, v) if inc is not None else None

    def dma(eng, out, in_, waits=(), sem=None):
        sem = sem if sem is not None else sL
        return mk(eng, lambda e, out=out, in_=in_: e.dma_start(out=out, in_=in_),
                  waits=waits, inc=sem, amt=16)

    def chunk(buf, cc, w=TT):
        return buf[:, cc * w:(cc + 1) * w]

    dma("sync", eyeb[:, :], eyeb_in)
    dma("sync", eyef[:, :], eyef_in)
    dma("sync", onesb[:, :], onesb_in)
    dma("sync", onesf[:, :], onesf_in)
    dma("sync", maskp[:, :], maskp_in)
    dma("sync", gq_raw[:, :], gq_in)
    dma("sync", gk_raw[:, :], gk_in)
    consts_ld = (sL, sL.n)
    mk("vector", lambda e: e.tensor_scalar_mul(gqm[:, :], gq_raw[:, :], MULT),
       waits=[consts_ld], inc=sV)
    mk("vector", lambda e: e.tensor_scalar_mul(gkm[:, :], gk_raw[:, :], MULT),
       waits=[consts_ld], inc=sV)
    gains_ready = (sV, sV.n)

    cast_marks = {}
    N0 = {"wg": NHT, "wu": NHT, "wd": NCH, "wq": NCH, "wk": G, "wv": NCH, "wo": NCH}

    def cast_weight(name, src, dst, nchunks):
        n0 = N0[name]
        step = n0 // nchunks
        marks = []
        for i in range(0, n0, step):
            dma("gpsimd", dst.ap()[i:i + step], src[i:i + step], sem=sW)
            marks.append((sW, sW.n))
        cast_marks[name] = marks

    def cast_interleaved():
        specs = {"wg": (wg_in, wg_b, 8), "wu": (wu_in, wu_b, 8)}
        marks = {"wg": [], "wu": []}
        for i in range(8):
            for name in ("wg", "wu"):
                srcT, dst, nch = specs[name]
                step = N0[name] // nch
                lo = i * step
                dma("gpsimd", dst.ap()[lo:lo + step], srcT[lo:lo + step], sem=sW)
                marks[name].append((sW, sW.n))
        cast_marks["wg"] = marks["wg"]
        cast_marks["wu"] = marks["wu"]

    cast_interleaved()
    cast_weight("wd", wd_in, wd_b, 4)
    cast_weight("wq", wq_in, wq_b, 2)
    cast_weight("wk", wk_in, wk_b, 1)
    cast_weight("wv", wv_in, wv_b, 1)
    cast_weight("wo", wo_in, wo_b, 2)

    def cast_ready(name, idx):
        marks = cast_marks[name]
        per = N0[name] // len(marks)
        return marks[min(idx // per, len(marks) - 1)]

    x_r = x_in.rearrange("(c p) t -> c p t", p=128)
    y_r = y_out.rearrange("(c p) t -> c p t", p=128)

    buf_last = {"xb": None, "wsa0": None, "wsa1": None, "wsb0": None,
                "wsb1": None, "wsd0": None, "wsd1": None, "ln_cast": None}

    def layernorm(src_f32, dst_bf16, extra_wait=()):
        cast_done = mk("scalar", lambda e: e.activation(xb[:, :], src_f32[:, :], AF.Copy),
                       waits=list(extra_wait) + [buf_last["xb"]], inc=sA)
        buf_last["ln_cast"] = cast_done
        for cc in range(NCH):
            mk("tensor",
               lambda e, cc=cc: e.matmul(psS[0][:, :], onesb[:, :], chunk(xb, cc),
                                         start=(cc == 0), stop=(cc == NCH - 1)),
               waits=[cast_done] if cc == 0 else (),
               inc=sP if cc == NCH - 1 else None)
        sum_done = (sP, sP.n)
        sq_done = mk("scalar", lambda e: e.activation(xb[:, :], xb[:, :], AF.Square),
                     waits=[sum_done], inc=sA)
        for cc in range(NCH):
            mk("tensor",
               lambda e, cc=cc: e.matmul(psA[4][0:1, :], onesb[:, :], chunk(xb, cc),
                                         start=(cc == 0), stop=(cc == NCH - 1)),
               waits=[sq_done] if cc == 0 else (),
               inc=sP if cc == NCH - 1 else None)
        sq_sum_done = (sP, sP.n)
        buf_last["xb"] = sq_sum_done
        mk("vector", lambda e: e.tensor_scalar_mul(m_row[:, :], psS[0][0:1, :], 1.0 / D),
           waits=[sum_done], inc=sV)
        mk("vector", lambda e: e.tensor_scalar_mul(v_row[:, :], psA[4][0:1, :], 1.0 / D),
           waits=[sq_sum_done], inc=sV)
        mk("scalar", lambda e: e.activation(r_row[:, :], m_row[:, :], AF.Square),
           waits=[(sV, sV.n)], inc=sA)
        mk("vector", lambda e: e.scalar_tensor_tensor(v_row[:, :], v_row[:, :], LN_EPS,
                                                      r_row[:, :], ALU.add, ALU.subtract),
           waits=[(sA, sA.n)], inc=sV)
        mk("scalar", lambda e: e.activation(r_row[:, :], v_row[:, :], AF.Sqrt),
           waits=[(sV, sV.n)], inc=sA)
        mk("vector", lambda e: e.reciprocal(r_row[:, :], r_row[:, :]),
           waits=[(sA, sA.n)], inc=sV)
        stats_done = mk("vector",
                        lambda e: e.scalar_tensor_tensor(n_row[:, :], m_row[:, :], -1.0,
                                                         r_row[:, :], ALU.mult, ALU.mult),
                        inc=sV)
        mk("tensor", lambda e: e.matmul(psB[0][:, :], onesf[:, :], r_row[:, :],
                                        start=True, stop=True),
           waits=[stats_done], inc=sP)
        bc_done = mk("tensor", lambda e: e.matmul(psB[1][:, :], onesf[:, :], n_row[:, :],
                                                  start=True, stop=True), inc=sP)
        marks = []
        for cc in range(NCH):
            mk("vector",
               lambda e, cc=cc: e.tensor_mul(ropeA[:, :], chunk(src_f32, cc), psB[0][:, :]),
               waits=[bc_done] if cc == 0 else ())
            marks.append(mk("vector",
                            lambda e, cc=cc: e.tensor_add(chunk(dst_bf16, cc), ropeA[:, :],
                                                          psB[1][:, :]),
                            inc=sV))
        return marks

    def ffn(src_bf16, dst_f32, scale, xln_marks):
        prev_gate_use = [None, None]
        prev_up_use = [None, None]
        for ht in range(NHT):
            par = ht % 2
            slab_g = wsa[:, par * 2048:(par + 1) * 2048]
            slab_u = wsb[:, par * 2048:(par + 1) * 2048]
            dma("sync", slab_g, wg_b.ap()[ht],
                waits=[cast_ready("wg", ht), buf_last[f"wsa{par}"]])
            g_ld = (sL, sL.n)
            dma("scalar", slab_u, wu_b.ap()[ht],
                waits=[cast_ready("wu", ht), buf_last[f"wsb{par}"]], sem=sSt)
            u_ld = (sSt, sSt.n)
            for cc in range(NCH):
                w = []
                if cc == 0:
                    w = [g_ld, prev_gate_use[par]]
                if ht == 0:
                    w.append(xln_marks[cc])
                mk("tensor",
                   lambda e, cc=cc, par=par, slab_g=slab_g: e.matmul(
                       psA[par][:, :], slab_g[:, cc * 128:(cc + 1) * 128],
                       chunk(src_bf16, cc), start=(cc == 0), stop=(cc == NCH - 1)),
                   waits=w, inc=sP if cc == NCH - 1 else None)
            gate_done = (sP, sP.n)
            buf_last[f"wsa{par}"] = gate_done
            for cc in range(NCH):
                mk("tensor",
                   lambda e, cc=cc, par=par, slab_u=slab_u: e.matmul(
                       psA[2 + par][:, :], slab_u[:, cc * 128:(cc + 1) * 128],
                       chunk(src_bf16, cc), start=(cc == 0), stop=(cc == NCH - 1)),
                   waits=[u_ld, prev_up_use[par]] if cc == 0 else (),
                   inc=sP if cc == NCH - 1 else None)
            up_done = (sP, sP.n)
            buf_last[f"wsb{par}"] = up_done
            sil_done = mk("scalar",
                          lambda e, par=par: e.activation(fsil[:, :], psA[par][:, :],
                                                          AF.Silu),
                          waits=[gate_done], inc=sA)
            prev_gate_use[par] = sil_done
            gmul = mk("vector",
                      lambda e, ht=ht, par=par: e.tensor_mul(
                          gated[:, ht * TT:(ht + 1) * TT], fsil[:, :], psA[2 + par][:, :]),
                      waits=[sil_done, up_done], inc=sV)
            prev_up_use[par] = gmul
        gated_done = (sV, sV.n)
        prev_down_use = list(prev_gate_use)
        down_marks = [None, None]
        res_done = None
        for dt in range(NCH):
            par = dt % 2
            dslab = wsd[:, par * 8192:(par + 1) * 8192]
            dma("sync", dslab, wd_b.ap()[dt],
                waits=[cast_ready("wd", dt), buf_last[f"wsd{par}"]])
            dslab_ld = (sL, sL.n)
            dwait = [dslab_ld, gated_done, prev_down_use[par]]
            for hc in range(NHT):
                mk("tensor",
                   lambda e, hc=hc, par=par, dslab=dslab: e.matmul(
                       psA[par][:, :], dslab[:, hc * 128:(hc + 1) * 128],
                       gated[:, hc * TT:(hc + 1) * TT], start=(hc == 0),
                       stop=(hc == NHT - 1)),
                   waits=dwait if hc == 0 else (),
                   inc=sP if hc == NHT - 1 else None)
            down_done = (sP, sP.n)
            down_marks[par] = down_done
            buf_last[f"wsd{par}"] = down_done
            res_done = mk("vector",
                          lambda e, dt=dt, par=par: e.scalar_tensor_tensor(
                              chunk(dst_f32, dt), psA[par][:, :], scale,
                              chunk(dst_f32, dt), ALU.mult, ALU.add),
                          waits=[down_done], inc=sV)
            prev_down_use[par] = res_done
        return res_done

    prev_xa_free = []
    for tt in range(NT):
        ts = slice(tt * TT, (tt + 1) * TT)
        dma("sync", xa.rearrange("p (c t) -> p c t", t=TT),
            x_in[:, ts].rearrange("(c p) t -> p c t", p=128),
            waits=list(prev_xa_free))
        x_ld = (sL, sL.n)
        dma("sync", cost[:, :], cos_in[:, ts], waits=[(sV, sV.n)])
        dma("sync", sint[:, :], sin_in[:, ts])
        trig_ld = (sL, sL.n)

        xln_marks = layernorm(xa, xln, extra_wait=[x_ld])
        y1_done = ffn(xln, xa, 0.5, xln_marks)
        dma("gpsimd", y1d.ap()[:, :, ts].rearrange("c p t -> p c t"),
            xa.rearrange("p (c t) -> p c t", t=TT),
            waits=[y1_done], sem=sSt)
        y1_spill = (sSt, sSt.n)
        y1ln_marks = layernorm(xa, xln, extra_wait=[y1_done])
        y1ln_done = y1ln_marks[-1]
        prev_xa_free = [y1_spill, buf_last["ln_cast"]]

        bank_guard = [None, None, None, None]
        bank = 0

        def qkv_proj(n_out, slab_ld_for, slab_ap_for, evac_for, ready, buf_key_for=None):
            nonlocal bank
            for i in range(n_out):
                slab_ld = slab_ld_for(i)
                bwait = [slab_ld, ready, bank_guard[bank]]
                for cc in range(NCH):
                    mk("tensor",
                       lambda e, i=i, cc=cc, bank=bank, slab_ap_for=slab_ap_for:
                       e.matmul(psA[bank][:, :], slab_ap_for(i, cc), chunk(xln, cc),
                                start=(cc == 0), stop=(cc == NCH - 1)),
                       waits=bwait if cc == 0 else (),
                       inc=sP if cc == NCH - 1 else None)
                mmd = (sP, sP.n)
                if buf_key_for is not None:
                    buf_last[buf_key_for(i)] = mmd
                bank_guard[bank] = evac_for(i, bank, mmd)
                bank = (bank + 1) % 4

        def q_slab_ld(qt):
            par = qt % 2
            dma("sync", wsa[:, par * 2048:(par + 1) * 2048], wq_b.ap()[qt],
                waits=[cast_ready("wq", qt), buf_last[f"wsa{par}"]])
            return (sL, sL.n)

        def q_slab_ap(qt, cc):
            par = qt % 2
            return wsa[:, par * 2048 + cc * 128: par * 2048 + (cc + 1) * 128]

        def q_evac(qt, bk, mmd):
            return mk("scalar",
                      lambda e, qt=qt, bk=bk: e.activation(
                          chunk(qf, qt), psA[bk][:, :], AF.Copy),
                      waits=[mmd, gains_ready], inc=sA)

        qkv_proj(NCH, q_slab_ld, q_slab_ap, q_evac, y1ln_done,
                 buf_key_for=lambda qt: f"wsa{qt % 2}")

        dma("sync", wsd[:, 0:8192].rearrange("p (g t) -> p g t", t=2048),
            wk_b.ap().rearrange("g p t -> p g t"),
            waits=[cast_ready("wk", 0), buf_last["wsd0"]])
        wk_ld = (sL, sL.n)

        def k_slab_ld(kt):
            return wk_ld

        def k_slab_ap(kt, cc):
            return wsd[:, kt * 2048 + cc * 128: kt * 2048 + (cc + 1) * 128]

        def k_evac(kt, bk, mmd):
            return mk("scalar",
                      lambda e, kt=kt, bk=bk: e.activation(
                          chunk(kf, kt), psA[bk][:, :], AF.Copy),
                      waits=[mmd], inc=sA)

        qkv_proj(G, k_slab_ld, k_slab_ap, k_evac, y1ln_done,
                 buf_key_for=lambda kt: "wsd0")

        dma("sync", wsd[:, 8192:8192 + NCH * 512].rearrange("p (c v) -> p c v", v=512),
            wv_b.ap().rearrange("c p v -> p c v"),
            waits=[cast_ready("wv", 0), buf_last["wsd1"]])
        wv_ld = (sL, sL.n)
        for sub in range(4):
            bwait = [wv_ld, y1ln_done, bank_guard[bank]]
            for cc in range(NCH):
                mk("tensor",
                   lambda e, cc=cc, sub=sub, bank=bank: e.matmul(
                       psA[bank][:, :],
                       xln[:, cc * TT + sub * 128: cc * TT + (sub + 1) * 128],
                       wsd[:, 8192 + cc * 512: 8192 + (cc + 1) * 512],
                       start=(cc == 0), stop=(cc == NCH - 1)),
                   waits=bwait if cc == 0 else (),
                   inc=sP if cc == NCH - 1 else None)
            mmd = (sP, sP.n)
            buf_last["wsd1"] = mmd
            bank_guard[bank] = mk("scalar",
                                  lambda e, sub=sub, bank=bank: e.activation(
                                      vtmp[:, sub * 512:(sub + 1) * 512],
                                      psA[bank][:, :], AF.Copy),
                                  waits=[mmd], inc=sA)
            bank = (bank + 1) % 4
        v_done = (sA, sA.n)

        rr_state = {"prev_mm": None, "prev_bc_use": None}

        def rms_rope(src, dst, n_chunks, first_wait, gain):
            prev_mm = rr_state["prev_mm"]
            prev_bc_use = rr_state["prev_bc_use"]
            last = None
            for ci in range(n_chunks):
                sch = chunk(src, ci)
                dch = chunk(dst, ci)
                sqd = mk("scalar",
                         lambda e, sch=sch: e.activation(sqbf[:, :], sch, AF.Square),
                         waits=[first_wait, prev_mm] if ci == 0 else [prev_mm],
                         inc=sA)
                ssum = mk("tensor",
                          lambda e: e.matmul(psS[0][:, :], onesb[:, :], sqbf[:, :],
                                             start=True, stop=True),
                          waits=[sqd], inc=sP)
                prev_mm = ssum
                mk("vector",
                   lambda e: e.tensor_scalar(m_row[:, :], psS[0][0:1, :], 1.0 / DH,
                                             RMS_EPS, ALU.mult, ALU.add),
                   waits=[ssum], inc=sV)
                rms = mk("scalar", lambda e: e.activation(m_row[:, :], m_row[:, :],
                                                          AF.Sqrt),
                         waits=[(sV, sV.n)], inc=sA)
                rcp = mk("vector", lambda e: e.reciprocal(m_row[:, :], m_row[:, :]),
                         waits=[rms], inc=sV)
                bc = mk("tensor", lambda e: e.matmul(psB[0][:, :], onesf[:, :],
                                                     m_row[:, :], start=True, stop=True),
                        waits=[rcp, prev_bc_use], inc=sP)
                qd = mk("vector",
                        lambda e, sch=sch: e.tensor_mul(qn[:, :], sch, psB[0][:, :]),
                        waits=[bc], inc=sV)
                prev_bc_use = qd
                mk("vector",
                   lambda e, ci=ci: e.tensor_scalar_mul(qn[:, :], qn[:, :],
                                                        gain[:, ci:ci + 1]))
                mk("vector", lambda e: e.tensor_copy(ropeA[0:64, :], qn[64:128, :]),
                   waits=[trig_ld])
                mk("vector", lambda e: e.tensor_copy(ropeA[64:128, :], qn[0:64, :]))
                mk("vector", lambda e: e.tensor_mul(ropeB[:, :], qn[:, :], cost[:, :]))
                mk("vector", lambda e: e.tensor_mul(ropeA[:, :], ropeA[:, :], sint[:, :]))
                last = mk("vector",
                          lambda e, dch=dch: e.tensor_add(dch, ropeB[:, :], ropeA[:, :]),
                          inc=sV)
            rr_state["prev_mm"] = prev_mm
            rr_state["prev_bc_use"] = prev_bc_use
            return last

        q_rope_done = rms_rope(qf, qkvb, NCH, (sA, sA.n), gqm)
        k_rope_done = rms_rope(kf, kb, G, (sA, sA.n), gkm)

        qkvb_r = qkvb.rearrange("p (c t) -> p c t", t=TT)
        for d in range(8):
            g2 = d % 4
            dma("gpsimd",
                q_a2a_i.ap()[d].rearrange("(c p) t -> p c t", p=128)[:, :, ts],
                qkvb_r[:, 4 * g2:4 * g2 + 4, :],
                waits=[q_rope_done], sem=sSt)
            dma("gpsimd", k_a2a_i.ap()[d][:, ts], kb[:, g2 * TT:(g2 + 1) * TT],
                waits=[k_rope_done], sem=sSt)
            dma("gpsimd",
                v_a2a_i.ap()[d][tt * TT:(tt + 1) * TT, :]
                .rearrange("(s p) v -> p s v", p=128),
                vtmp.rearrange("p (s v) -> p s v", v=512)[:, :, g2 * 128:(g2 + 1) * 128],
                waits=[v_done], sem=sSt)

    groups8 = [list(range(8))]

    def coll(ins_ap, outs_ap, waits):
        def fn(e, ins_ap=ins_ap, outs_ap=outs_ap,
               waits=tuple(x for x in waits if x is not None)):
            for (s, val) in waits:
                e.wait_ge(s.h, val)
            with allow_ici_all_to_all():
                e.collective_compute("AllToAll", ALU.bypass, replica_groups=groups8,
                                     ins=[ins_ap], outs=[outs_ap]).then_inc(sC.h, 1)
        sC.inc(1)
        plans["gpsimd"].append(fn)
        return (sC, sC.n)

    a2a1_wait = [(sSt, sSt.n)]
    coll(q_a2a_i.ap().rearrange("s q t -> (s q) t"),
         q_a2a_o.ap().rearrange("s q t -> (s q) t"), a2a1_wait)
    coll(k_a2a_i.ap().rearrange("s q t -> (s q) t"),
         k_a2a_o.ap().rearrange("s q t -> (s q) t"), [])
    a2a1_done = coll(v_a2a_i.ap().rearrange("s q t -> (s q) t"),
                     v_a2a_o.ap().rearrange("s q t -> (s q) t"), [])

    def switch_copy(pairs_fn, waits, n_dmas, sem):
        v = sem.inc(16 * n_dmas)

        def fn(e, waits=tuple(x for x in waits if x is not None)):
            for (s, val) in waits:
                e.wait_ge(s.h, val)
            pid = e.partition_id()
            for i in nc.Switch(engines=[e], index=[pid], n=8):
                prs = pairs_fn(i)
                assert len(prs) == n_dmas
                for (o, inp) in prs:
                    e.dma_start(out=o, in_=inp).then_inc(sem.h, 16)
        plans["sync"].append(fn)
        return (sem, v)

    def qkv_pairs(core):
        b = core // 4
        out = []
        for r in range(4):
            out.append((q_mine.ap()[r], q_a2a_o.ap()[4 * b + r]))
            out.append((k_sb[:, r * TOK:(r + 1) * TOK], k_a2a_o.ap()[4 * b + r]))
            out.append((v_sb[:, r * TOK:(r + 1) * TOK].rearrange("p (j d) -> p j d", d=128),
                        v_a2a_o.ap()[4 * b + r].rearrange("(j p) d -> p j d", p=128)))
        return out

    kv_done = switch_copy(qkv_pairs, [a2a1_done], 12, sL)
    qmine_done = kv_done

    qh_marks = {}
    for h in range(2):
        dma("sync", q_sbs[h].rearrange("p (r t) -> p r t", r=4),
            q_mine.ap()[:, h * 128:(h + 1) * 128, :].rearrange("r p t -> p r t"),
            waits=[qmine_done])
        qh_marks[h] = (sL, sL.n)

    P1B = [0, 1, 4]
    p1_guard = [None, None, None]
    p2_guard = [None, None]
    prev_o_evac = None
    prev_sum_cp = None
    prev_obuf_store = [None, None]
    prev_sum_store = [None, None]
    prev_mrow = None
    prev_negM_rd = {}
    negM_marks = {}
    att_tail = []

    def emit_pass2(h, qb):
        nonlocal prev_o_evac, prev_sum_cp
        nj = 4 * qb + 4
        qs = slice(qb * QB, (qb + 1) * QB)
        qsb = q_sbs[h % 2]
        negM_ap = negM[:, qb % 2:qb % 2 + 1]
        pt_rd = {}
        first_exp = None
        last_exp = None
        for jt in range(nj):
            par = jt % 2
            diag_r = jt - 4 * qb
            w0 = [p2_guard[par]]
            mk("tensor",
               lambda e, jt=jt, par=par, qs=qs, qsb=qsb, diag=(diag_r >= 0):
               e.matmul(psA[2 + par][:, :], k_sb[:, jt * 128:(jt + 1) * 128],
                        qsb[:, qs], start=True, stop=(not diag)),
               waits=w0,
               inc=None if diag_r >= 0 else sP)
            if diag_r >= 0:
                mk("tensor",
                   lambda e, par=par, r=diag_r: e.matmul(
                       psA[2 + par][:, :], eyeb[:, :],
                       maskp[:, r * 512:(r + 1) * 512], start=False, stop=True),
                   inc=sP)
            st2 = (sP, sP.n)
            ewaits = [st2, pt_rd.get(jt - 4)]
            if jt == 0:
                ewaits.append(negM_marks[(h, qb)])
            ex = mk("scalar",
                    lambda e, jt=jt, par=par, negM_ap=negM_ap: e.activation(
                        pts[jt % 4][:, :], psA[2 + par][:, :], AF.Exp, bias=negM_ap),
                    waits=ewaits, inc=sA)
            p2_guard[par] = ex
            last_exp = ex
            if first_exp is None:
                first_exp = ex
            pvw = [ex]
            if jt == 0:
                pvw.append(prev_o_evac)
            mk("tensor",
               lambda e, jt=jt: e.matmul(psB[0][:, :], v_sb[:, jt * 128:(jt + 1) * 128],
                                         pts[jt % 4][:, :], start=(jt == 0),
                                         stop=(jt == nj - 1)),
               waits=pvw, inc=sP if jt == nj - 1 else None)
            sw = []
            if jt == 0:
                sw.append(prev_sum_cp)
            mk("tensor",
               lambda e, jt=jt: e.matmul(psS[0][:, :], onesb[:, :], pts[jt % 4][:, :],
                                         start=(jt == 0), stop=(jt == nj - 1)),
               waits=sw, inc=sP if jt == nj - 1 else None)
            pt_rd[jt] = (sP, sP.n)
        prev_negM_rd[(h, qb)] = last_exp
        pv_done = (sP, sP.n)
        blk = (h * NQB + qb) % 2
        o_evac = mk("scalar",
                    lambda e, blk=blk: e.activation(obufs[blk][:, :], psB[0][:, :],
                                                    AF.Copy),
                    waits=[pv_done, prev_obuf_store[blk]], inc=sA)
        prev_o_evac = o_evac
        sum_cp = mk("vector",
                    lambda e, blk=blk: e.tensor_copy(sumrows[blk][:, :], psS[0][0:1, :]),
                    waits=[pv_done, prev_sum_store[blk]], inc=sV)
        prev_sum_cp = sum_cp
        quarter = qb // 4
        cslice = slice((qb % 4) * 512, (qb % 4) * 512 + 512)
        for d in (quarter, quarter + 4):
            dma("gpsimd", o_a2a_i.ap()[d][h * 128:(h + 1) * 128, cslice],
                obufs[blk][:, :], waits=[o_evac], sem=sSt)
        prev_obuf_store[blk] = (sSt, sSt.n)
        dma("gpsimd", s_scr.ap()[h][qb:qb + 1, :], sumrows[blk][:, :],
            waits=[sum_cp], sem=sSt)
        prev_sum_store[blk] = (sSt, sSt.n)

    for h in range(HPG):
        q_ld = qh_marks[h]
        qsb = q_sbs[h % 2]
        for qb in range(NQB):
            nj = 4 * qb + 4
            qs = slice(qb * QB, (qb + 1) * QB)
            for jt in range(nj):
                par = jt % 3
                bk = P1B[par]
                diag_r = jt - 4 * qb
                w0 = [p1_guard[par]]
                if jt == 0:
                    w0 += [q_ld, kv_done]
                mk("tensor",
                   lambda e, jt=jt, bk=bk, qs=qs, qsb=qsb, diag=(diag_r >= 0):
                   e.matmul(psA[bk][:, :], k_sb[:, jt * 128:(jt + 1) * 128],
                            qsb[:, qs], start=True, stop=(not diag)),
                   waits=w0, inc=None if diag_r >= 0 else sP)
                if diag_r >= 0:
                    mk("tensor",
                       lambda e, bk=bk, r=diag_r: e.matmul(
                           psA[bk][:, :], eyeb[:, :],
                           maskp[:, r * 512:(r + 1) * 512], start=False, stop=True),
                       inc=sP)
                st_done = (sP, sP.n)
                p1_guard[par] = mk("vector",
                                   lambda e, jt=jt, bk=bk: e.reduce_max(
                                       maxbuf[:, jt:jt + 1], psA[bk][:, :], axis=AX.X),
                                   waits=[st_done], inc=sV)
            if qb > 0:
                emit_pass2(h, qb - 1)
            mcol = mk("vector",
                      lambda e, nj=nj: e.reduce_max(maxcol[:, :], maxbuf[:, 0:nj],
                                                    axis=AX.X),
                      inc=sV)
            tr = mk("tensor",
                    lambda e: e.transpose(psB[1][0:1, 0:128], maxcol[:, :], eyef[:, :]),
                    waits=[mcol, prev_mrow], inc=sP)
            mrow = mk("vector",
                      lambda e: e.reduce_max(srow[0:1, 0:1], psB[1][0:1, 0:128],
                                             axis=AX.X, negate=True),
                      waits=[tr], inc=sV)
            prev_mrow = mrow
            bcM = mk("tensor",
                     lambda e: e.matmul(psB[1][:, 256:257], onesf[:, :], srow[0:1, 0:1],
                                        start=True, stop=True),
                     waits=[mrow, negM_marks.get((h, qb - 1))], inc=sP)
            negM_marks[(h, qb)] = mk("vector",
                                     lambda e, qb=qb: e.tensor_copy(
                                         negM[:, qb % 2:qb % 2 + 1], psB[1][:, 256:257]),
                                     waits=[bcM, prev_negM_rd.get((h, qb - 2))],
                                     inc=sV)
        emit_pass2(h, NQB - 1)
        if h + 2 < HPG:
            dma("sync", q_sbs[h % 2].rearrange("p (r t) -> p r t", r=4),
                q_mine.ap()[:, (h + 2) * 128:(h + 3) * 128, :].rearrange("r p t -> p r t"),
                waits=[(sP, sP.n)])
            qh_marks[h + 2] = (sL, sL.n)

    att_stores = (sSt, sSt.n)
    for d in range(8):
        g2 = d % 4
        dma("gpsimd", s_a2a_i.ap()[d],
            s_scr.ap().rearrange("h q b -> h (q b)")[:, g2 * TOK:(g2 + 1) * TOK],
            waits=[att_stores], sem=sSt)

    coll(o_a2a_i.ap().rearrange("s q t -> (s q) t"),
         o_a2a_o.ap().rearrange("s q t -> (s q) t"), [(sSt, sSt.n)])
    a2a2_done = coll(s_a2a_i.ap().rearrange("s q t -> (s q) t"),
                     s_a2a_o.ap().rearrange("s q t -> (s q) t"), [])

    def o_pairs(core):
        b = core // 4
        out = []
        om = o_mine.ap().rearrange("(g x) p t -> g (x p) t", g=4)
        sm = s_mine.ap().rearrange("(g h) t -> g h t", g=4)
        for g2 in range(4):
            out.append((om[g2], o_a2a_o.ap()[4 * b + g2]))
            out.append((sm[g2], s_a2a_o.ap()[4 * b + g2]))
        return out

    omine_done = switch_copy(o_pairs, [a2a2_done], 8, sL)

    for tt in range(NT):
        ts = slice(tt * TT, (tt + 1) * TT)
        gated_free = [buf_last["wsd0"], buf_last["wsd1"]]
        if tt == 0:
            gated_free = [omine_done, (sP, sP.n), (sV, sV.n), (sA, sA.n)]
        dma("sync", ob.rearrange("p (c t) -> p c t", t=TT),
            o_mine.ap()[:, :, ts].rearrange("c p t -> p c t"),
            waits=gated_free + [omine_done])
        dma("sync", sraw[0:1, :].rearrange("a (c t) -> a c t", c=16),
            s_mine.ap().rearrange("(a c) t -> a c t", a=1)[:, :, ts])
        dma("sync", xa.rearrange("p (c t) -> p c t", t=TT),
            y1d.ap()[:, :, ts].rearrange("c p t -> p c t"),
            waits=(gated_free if tt == 0 else list(prev_xa_free)))
        c_ld = (sL, sL.n)
        rcp = mk("vector", lambda e: e.reciprocal(sraw[:, :], sraw[:, :]),
                 waits=[c_ld], inc=sV)
        prev_norm = [None, None]
        norm_done = None
        for cc in range(NCH):
            par = cc % 2
            bc = mk("tensor",
                    lambda e, cc=cc, par=par: e.matmul(
                        psB[par][:, :], onesf[:, :], sraw[0:1, cc * TT:(cc + 1) * TT],
                        start=True, stop=True),
                    waits=[rcp, prev_norm[par]], inc=sP)
            norm_done = mk("vector",
                           lambda e, cc=cc, par=par: e.tensor_mul(
                               chunk(ob, cc), chunk(ob, cc), psB[par][:, :]),
                           waits=[bc], inc=sV)
            prev_norm[par] = norm_done
        prev_res = [None, None]
        y2_done = None
        for dt in range(NCH):
            par = dt % 2
            dma("sync", wsa[:, par * 2048:(par + 1) * 2048], wo_b.ap()[dt],
                waits=[cast_ready("wo", dt), buf_last[f"wsa{par}"]])
            slab_ld = (sL, sL.n)
            mwait = [slab_ld, norm_done, prev_res[par]]
            for cc in range(NCH):
                mk("tensor",
                   lambda e, dt=dt, cc=cc, par=par: e.matmul(
                       psA[par][:, :],
                       wsa[:, par * 2048 + cc * 128: par * 2048 + (cc + 1) * 128],
                       chunk(ob, cc), start=(cc == 0), stop=(cc == NCH - 1)),
                   waits=mwait if cc == 0 else (),
                   inc=sP if cc == NCH - 1 else None)
            mmd = (sP, sP.n)
            buf_last[f"wsa{par}"] = mmd
            y2_done = mk("vector",
                         lambda e, dt=dt, par=par: e.tensor_add(
                             chunk(xa, dt), psA[par][:, :], chunk(xa, dt)),
                         waits=[mmd], inc=sV)
            prev_res[par] = y2_done
        if y2d is not None:
            dma("gpsimd", y2d.ap()[:, :, ts].rearrange("c p t -> p c t"),
                xa.rearrange("p (c t) -> p c t", t=TT), waits=[y2_done], sem=sSt)
        if obnd is not None:
            dma("gpsimd", obnd.ap()[:, :, ts].rearrange("c p t -> p c t"),
                ob.rearrange("p (c t) -> p c t", t=TT), waits=[norm_done], sem=sSt)
        y2ln_marks = layernorm(xa, xln, extra_wait=[y2_done])
        y3_done = ffn(xln, xa, 0.5, y2ln_marks)
        dma("gpsimd", y_r[:, :, ts].rearrange("c p t -> p c t"),
            xa.rearrange("p (c t) -> p c t", t=TT),
            waits=[y3_done], sem=sO)
        prev_xa_free = [(sO, sO.n), buf_last["ln_cast"]]

    tap_map = {
        "y1": (y1d, [NCH, 128, TOK], F32),
        "y2": (y2d, [NCH, 128, TOK], F32),
        "obn": (obnd, [NCH, 128, TOK], BF16),
        "qin": (q_a2a_i, [8, 512, TOK], BF16),
        "kin": (k_a2a_i, [8, 128, TOK], BF16),
        "vin": (v_a2a_i, [8, TOK, 128], BF16),
        "omine": (o_mine, [NCH, 128, TOK], BF16),
        "smine": (s_mine, [16, TOK], F32),
    }
    for tname in taps:
        src, shape, dt = tap_map[tname]
        tout = nc.dram_tensor("t_" + tname, shape, dt, kind="ExternalOutput").ap()
        dma("gpsimd", tout, src.ap(), waits=[(sO, sO.n), (sSt, sSt.n)], sem=sO)

    fin = (sO, sO.n)

    def fin_fn(e, fin=fin):
        e.wait_ge(fin[0].h, fin[1])
    plans["gpsimd"].append(fin_fn)

    with nc.Block() as block:
        @block.sync
        def _(e):
            for fn in plans["sync"]:
                fn(e)

        @block.tensor
        def _(e):
            for fn in plans["tensor"]:
                fn(e)

        @block.vector
        def _(e):
            for fn in plans["vector"]:
                fn(e)

        @block.scalar
        def _(e):
            for fn in plans["scalar"]:
                fn(e)

        @block.gpsimd
        def _(e):
            for fn in plans["gpsimd"]:
                fn(e)

    stk.close()
    return nc


def _host_inputs(x, wq, wk, wv, wo, gq, gk, w_gate, w_up, w_down):
    bf = ml_dtypes.bfloat16
    wg_sw = np.ascontiguousarray(
        w_gate.reshape(NHT, 128, NCH, 128).transpose(0, 3, 2, 1)).reshape(NHT, 128, 2048)
    wu_sw = np.ascontiguousarray(
        w_up.reshape(NHT, 128, NCH, 128).transpose(0, 3, 2, 1)).reshape(NHT, 128, 2048)
    wd_sw = np.ascontiguousarray(
        w_down.reshape(NCH, 128, NHT, 128).transpose(0, 3, 2, 1)).reshape(NCH, 128, 8192)
    wq_sw = np.ascontiguousarray(
        wq.reshape(NCH, 128, NCH, 128).transpose(2, 1, 0, 3)).reshape(NCH, 128, 2048)
    wk_sw = np.ascontiguousarray(
        wk.reshape(NCH, 128, G, 128).transpose(2, 1, 0, 3)).reshape(G, 128, 2048)
    wv_n = np.ascontiguousarray(wv.reshape(NCH, 128, 512))
    wo_sw = np.ascontiguousarray(
        wo.reshape(NCH, 128, NCH, 128).transpose(2, 1, 0, 3)).reshape(NCH, 128, 2048)
    gq_t = np.ascontiguousarray(gq.reshape(16, 128).T)
    gk_t = np.ascontiguousarray(gk.T)

    dims = np.arange(64, dtype=np.float64)
    ang = np.power(10000.0, -dims / 64.0)
    pos = np.arange(T, dtype=np.float64)
    rad = (pos[None, :] * ang[:, None]).astype(np.float32)
    c64 = np.cos(rad).astype(np.float32)
    s64 = np.sin(rad).astype(np.float32)
    cosw = np.concatenate([c64, c64], axis=0)
    sinw = np.concatenate([-s64, s64], axis=0)

    eyeb = np.eye(128, dtype=bf)
    eyef = np.eye(128, dtype=np.float32)
    onesb = np.ones((128, 1), dtype=bf)
    onesf = np.ones((1, 128), dtype=np.float32)
    maskp = np.zeros((128, 4 * 512), dtype=np.float32)
    jj = np.arange(128)[:, None]
    qq = np.arange(512)[None, :]
    for r in range(4):
        maskp[:, r * 512:(r + 1) * 512] = np.where(qq < jj + 128 * r, NEG_BIG, 0.0)
    maskp = maskp.astype(bf)

    shared = dict(wg=wg_sw, wu=wu_sw, wd=wd_sw, wq=wq_sw, wk=wk_sw, wv=wv_n,
                  wo=wo_sw, gq=gq_t, gk=gk_t, eyeb=eyeb, eyef=eyef,
                  onesb=onesb, onesf=onesf, maskp=maskp)

    in_maps = []
    for c in range(8):
        b, q4 = c // 4, c % 4
        t0 = q4 * TOK
        im = dict(shared)
        im["x"] = np.ascontiguousarray(x[b, t0:t0 + TOK, :].T)
        im["cosw"] = np.ascontiguousarray(cosw[:, t0:t0 + TOK])
        im["sinw"] = np.ascontiguousarray(sinw[:, t0:t0 + TOK])
        in_maps.append(im)
    return in_maps


def kernel(L0_attn_x, L0_attn_wq, L0_attn_wk, L0_attn_wv, L0_attn_wo,
           L0_attn_gq, L0_attn_gk, L0_ffn_w_gate, L0_ffn_w_up, L0_ffn_w_down):
    x = np.asarray(L0_attn_x, dtype=np.float32)
    in_maps = _host_inputs(x,
                           np.asarray(L0_attn_wq, np.float32),
                           np.asarray(L0_attn_wk, np.float32),
                           np.asarray(L0_attn_wv, np.float32),
                           np.asarray(L0_attn_wo, np.float32),
                           np.asarray(L0_attn_gq, np.float32),
                           np.asarray(L0_attn_gk, np.float32),
                           np.asarray(L0_ffn_w_gate, np.float32),
                           np.asarray(L0_ffn_w_up, np.float32),
                           np.asarray(L0_ffn_w_down, np.float32))
    nc = build_nc()
    res = run_bass_kernel_spmd(nc, in_maps, core_ids=list(range(8)))
    out = np.empty((NB, T, D), dtype=np.float32)
    for c in range(8):
        b, q4 = c // 4, c % 4
        t0 = q4 * TOK
        out[b, t0:t0 + TOK, :] = res.results[c]["y"].T
    return out


# version 08
# speedup vs baseline: 1.1895x; 1.1480x over previous
import contextlib

import numpy as np
import ml_dtypes

import concourse.bass as bass
import concourse.mybir as mybir
from concourse.bass_utils import run_bass_kernel_spmd
from concourse.aot_env import allow_ici_all_to_all

F32 = mybir.dt.float32
BF16 = mybir.dt.bfloat16
AF = mybir.ActivationFunctionType
ALU = mybir.AluOpType
AX = mybir.AxisListType

D = 2048
NB = 2
HID = 8192
T = 8192
TOK = 2048
TT = 512
NT = TOK // TT
NCH = D // 128
NHT = HID // 128
G = 4
HPG = 4
DH = 128
QB = 512
NQB = T // QB
NJT = T // 128
LN_EPS = 1e-5
RMS_EPS = 1e-6
MULT = float(DH) ** -0.25
NEG_BIG = -1e30


class Sem:
    def __init__(self, h):
        self.h = h
        self.n = 0

    def inc(self, k):
        self.n += k
        return self.n


def build_nc(taps=()):
    nc = bass.Bass("TRN2", target_bir_lowering=False, debug=False)

    def din(name, shape, dt=F32):
        return nc.dram_tensor(name, shape, dt, kind="ExternalInput").ap()

    x_in = din("x", [D, TOK])
    wg_in = din("wg", [NHT, 128, 2048])
    wu_in = din("wu", [NHT, 128, 2048])
    wd_in = din("wd", [NCH, 128, 8192])
    wq_in = din("wq", [NCH, 128, 2048])
    wk_in = din("wk", [G, 128, 2048])
    wv_in = din("wv", [NCH, 128, 512])
    wo_in = din("wo", [NCH, 128, 2048])
    gq_in = din("gq", [128, 16])
    gk_in = din("gk", [128, 4])
    cos_in = din("cosw", [128, TOK])
    sin_in = din("sinw", [128, TOK])
    eyeb_in = din("eyeb", [128, 128], BF16)
    eyef_in = din("eyef", [128, 128], F32)
    onesb_in = din("onesb", [128, 1], BF16)
    onesf_in = din("onesf", [1, 128], F32)
    maskp_in = din("maskp", [128, 4 * 512], BF16)

    y_out = nc.dram_tensor("y", [D, TOK], F32, kind="ExternalOutput").ap()

    def dint(name, shape, dt):
        return nc.dram_tensor(name, shape, dt, kind="Internal")

    wg_b = dint("wg_b", [NHT, 128, 2048], BF16)
    wu_b = dint("wu_b", [NHT, 128, 2048], BF16)
    wd_b = dint("wd_b", [NCH, 128, 8192], BF16)
    wq_b = dint("wq_b", [NCH, 128, 2048], BF16)
    wk_b = dint("wk_b", [G, 128, 2048], BF16)
    wv_b = dint("wv_b", [NCH, 128, 512], BF16)
    wo_b = dint("wo_b", [NCH, 128, 2048], BF16)

    y1d = dint("y1d", [NCH, 128, TOK], F32)

    q_a2a_i = dint("q_a2a_i", [8, 512, TOK], BF16)
    q_a2a_o = dint("q_a2a_o", [8, 512, TOK], BF16)
    k_a2a_i = dint("k_a2a_i", [8, 128, TOK], BF16)
    k_a2a_o = dint("k_a2a_o", [8, 128, TOK], BF16)
    v_a2a_i = dint("v_a2a_i", [8, TOK, 128], BF16)
    v_a2a_o = dint("v_a2a_o", [8, TOK, 128], BF16)
    o_a2a_i = dint("o_a2a_i", [8, 512, TOK], BF16)
    o_a2a_o = dint("o_a2a_o", [8, 512, TOK], BF16)
    s_a2a_i = dint("s_a2a_i", [8, 4, TOK], F32)
    s_a2a_o = dint("s_a2a_o", [8, 4, TOK], F32)

    q_mine = dint("q_mine", [4, 512, TOK], BF16)
    o_mine = dint("o_mine", [NCH, 128, TOK], BF16)
    s_mine = dint("s_mine", [16, TOK], F32)
    s_scr = dint("s_scr", [4, NQB, QB], F32)
    y2d = dint("y2d", [NCH, 128, TOK], F32) if "y2" in taps else None
    obnd = dint("obnd", [NCH, 128, TOK], BF16) if "obn" in taps else None

    eyeb = nc.alloc_sbuf_tensor("eyeb_s", [128, 128], BF16)
    eyef = nc.alloc_sbuf_tensor("eyef_s", [128, 128], F32)
    onesb = nc.alloc_sbuf_tensor("onesb_s", [128, 1], BF16)
    onesf = nc.alloc_sbuf_tensor("onesf_s", [1, 128], F32)
    maskp = nc.alloc_sbuf_tensor("maskp_s", [128, 4 * 512], BF16)
    gq_raw = nc.alloc_sbuf_tensor("gq_raw", [128, 16], F32)
    gk_raw = nc.alloc_sbuf_tensor("gk_raw", [128, 4], F32)
    gqm = nc.alloc_sbuf_tensor("gqm_s", [128, 16], F32)
    gkm = nc.alloc_sbuf_tensor("gkm_s", [128, 4], F32)
    cost = nc.alloc_sbuf_tensor("cost", [128, TT], F32)
    sint = nc.alloc_sbuf_tensor("sint", [128, TT], F32)
    m_row = nc.alloc_sbuf_tensor("m_row", [1, TT], F32)
    v_row = nc.alloc_sbuf_tensor("v_row", [1, TT], F32)
    r_row = nc.alloc_sbuf_tensor("r_row", [1, TT], F32)
    n_row = nc.alloc_sbuf_tensor("n_row", [1, TT], F32)
    srow = nc.alloc_sbuf_tensor("srow", [1, 8], F32)
    sumrows = [nc.alloc_sbuf_tensor("sumrow0", [1, QB], F32),
               nc.alloc_sbuf_tensor("sumrow1", [1, QB], F32)]
    negM = nc.alloc_sbuf_tensor("negM", [128, 2], F32)
    maxcol = nc.alloc_sbuf_tensor("maxcol", [128, 1], F32)
    maxbuf = nc.alloc_sbuf_tensor("maxbuf", [128, NJT], F32)
    sqbf = nc.alloc_sbuf_tensor("sqbf", [128, TT], BF16)

    base = (nc.sbuf_base + 31) & ~31

    def sb_at(name, shape, dt, off):
        return nc.alloc_sbuf_tensor_at(name, shape, dt, offset=base + off)

    xa = sb_at("xa", [128, NCH * TT], F32, 0)
    xb = sb_at("xb", [128, NCH * TT], BF16, 32768)
    xln = sb_at("xln", [128, NCH * TT], BF16, 49152)
    gated = sb_at("gated", [128, NHT * TT], BF16, 65536)
    wsa = sb_at("wsa", [128, 2 * 2048], BF16, 131072)
    wsb = sb_at("wsb", [128, 2 * 2048], BF16, 139264)
    wsd = sb_at("wsd", [128, 2 * 8192], BF16, 147456)
    fsil = sb_at("fsil", [128, 512], F32, 180224)
    ropeA = sb_at("ropeA", [128, TT], F32, 182272)
    ropeB = sb_at("ropeB", [128, TT], F32, 184320)
    END_OFF = 186368
    assert base + END_OFF <= nc.sbuf_top, (base, END_OFF, nc.sbuf_top)

    qf = sb_at("qf", [128, NCH * TT], F32, 65536)
    kf = sb_at("kf", [128, G * TT], F32, 65536 + 32768)
    vtmp = sb_at("vtmp", [128, 4 * 512], BF16, 65536 + 40960)
    qkvb = sb_at("qkvb", [128, NCH * TT], BF16, 65536 + 45056)
    kb = sb_at("kb", [128, G * TT], BF16, 65536 + 61440)
    ob = sb_at("ob", [128, NCH * TT], BF16, 65536)
    sraw = sb_at("sraw", [1, 16 * TT], F32, 65536 + 16384)
    qn = sb_at("qn", [128, TT], F32, 180224)

    k_sb = sb_at("k_sb", [128, T], BF16, 0)
    v_sb = sb_at("v_sb", [128, NJT * 128], BF16, 16384)
    q_sbs = [sb_at("q_sb0", [128, T], BF16, 32768),
             sb_at("q_sb1", [128, T], BF16, 49152)]
    pts = [sb_at("pt0", [128, QB], BF16, 65536),
           sb_at("pt1", [128, QB], BF16, 66560),
           sb_at("pt2", [128, QB], BF16, 67584),
           sb_at("pt3", [128, QB], BF16, 68608)]
    obufs = [sb_at("obuf0", [128, QB], BF16, 69632),
             sb_at("obuf1", [128, QB], BF16, 70656)]

    psA = [nc.alloc_psum_tensor(f"psA{i}", [128, 512], F32) for i in range(5)]
    psS = [nc.alloc_psum_tensor("psS0", [1, 512], F32)]
    psB = [nc.alloc_psum_tensor(f"psB{i}", [128, 512], F32) for i in range(2)]

    stk = contextlib.ExitStack()
    sW = Sem(stk.enter_context(nc.semaphore("sW")))
    sL = Sem(stk.enter_context(nc.semaphore("sL")))
    sSt = Sem(stk.enter_context(nc.semaphore("sSt")))
    sP = Sem(stk.enter_context(nc.semaphore("sP")))
    sV = Sem(stk.enter_context(nc.semaphore("sV")))
    sA = Sem(stk.enter_context(nc.semaphore("sA")))
    sC = Sem(stk.enter_context(nc.semaphore("sC")))
    sO = Sem(stk.enter_context(nc.semaphore("sO")))

    plans = {k: [] for k in ["sync", "tensor", "vector", "scalar", "gpsimd"]}

    def mk(eng, build, waits=(), inc=None, amt=1):
        v = inc.inc(amt) if inc is not None else None
        w = tuple(x for x in waits if x is not None)

        def fn(e, build=build, w=w, inc=inc, amt=amt):
            for (s, val) in w:
                e.wait_ge(s.h, val)
            ins = build(e)
            if inc is not None:
                ins.then_inc(inc.h, amt)
        plans[eng].append(fn)
        return (inc, v) if inc is not None else None

    def dma(eng, out, in_, waits=(), sem=None):
        sem = sem if sem is not None else sL
        return mk(eng, lambda e, out=out, in_=in_: e.dma_start(out=out, in_=in_),
                  waits=waits, inc=sem, amt=16)

    def chunk(buf, cc, w=TT):
        return buf[:, cc * w:(cc + 1) * w]

    dma("sync", eyeb[:, :], eyeb_in)
    dma("sync", eyef[:, :], eyef_in)
    dma("sync", onesb[:, :], onesb_in)
    dma("sync", onesf[:, :], onesf_in)
    dma("sync", maskp[:, :], maskp_in)
    dma("sync", gq_raw[:, :], gq_in)
    dma("sync", gk_raw[:, :], gk_in)
    consts_ld = (sL, sL.n)
    mk("vector", lambda e: e.tensor_scalar_mul(gqm[:, :], gq_raw[:, :], MULT),
       waits=[consts_ld], inc=sV)
    mk("vector", lambda e: e.tensor_scalar_mul(gkm[:, :], gk_raw[:, :], MULT),
       waits=[consts_ld], inc=sV)
    gains_ready = (sV, sV.n)

    cast_marks = {}
    N0 = {"wg": NHT, "wu": NHT, "wd": NCH, "wq": NCH, "wk": G, "wv": NCH, "wo": NCH}

    def cast_weight(name, src, dst, nchunks):
        n0 = N0[name]
        step = n0 // nchunks
        marks = []
        for i in range(0, n0, step):
            dma("gpsimd", dst.ap()[i:i + step], src[i:i + step], sem=sW)
            marks.append((sW, sW.n))
        cast_marks[name] = marks

    def cast_interleaved():
        specs = {"wg": (wg_in, wg_b, 8), "wu": (wu_in, wu_b, 8)}
        marks = {"wg": [], "wu": []}
        for i in range(8):
            for name in ("wg", "wu"):
                srcT, dst, nch = specs[name]
                step = N0[name] // nch
                lo = i * step
                dma("gpsimd", dst.ap()[lo:lo + step], srcT[lo:lo + step], sem=sW)
                marks[name].append((sW, sW.n))
        cast_marks["wg"] = marks["wg"]
        cast_marks["wu"] = marks["wu"]

    cast_interleaved()
    cast_weight("wd", wd_in, wd_b, 4)
    cast_weight("wq", wq_in, wq_b, 2)
    cast_weight("wk", wk_in, wk_b, 1)
    cast_weight("wv", wv_in, wv_b, 1)
    cast_weight("wo", wo_in, wo_b, 2)

    def cast_ready(name, idx):
        marks = cast_marks[name]
        per = N0[name] // len(marks)
        return marks[min(idx // per, len(marks) - 1)]

    x_r = x_in.rearrange("(c p) t -> c p t", p=128)
    y_r = y_out.rearrange("(c p) t -> c p t", p=128)

    buf_last = {"xb": None, "wsa0": None, "wsa1": None, "wsb0": None,
                "wsb1": None, "wsd0": None, "wsd1": None, "ln_cast": None}

    def layernorm(src_f32, dst_bf16, extra_wait=()):
        cast_done = mk("scalar", lambda e: e.activation(xb[:, :], src_f32[:, :], AF.Copy),
                       waits=list(extra_wait) + [buf_last["xb"]], inc=sA)
        buf_last["ln_cast"] = cast_done
        for cc in range(NCH):
            mk("tensor",
               lambda e, cc=cc: e.matmul(psS[0][:, :], onesb[:, :], chunk(xb, cc),
                                         start=(cc == 0), stop=(cc == NCH - 1)),
               waits=[cast_done] if cc == 0 else (),
               inc=sP if cc == NCH - 1 else None)
        sum_done = (sP, sP.n)
        sq_done = mk("scalar", lambda e: e.activation(xb[:, :], xb[:, :], AF.Square),
                     waits=[sum_done], inc=sA)
        for cc in range(NCH):
            mk("tensor",
               lambda e, cc=cc: e.matmul(psA[4][0:1, :], onesb[:, :], chunk(xb, cc),
                                         start=(cc == 0), stop=(cc == NCH - 1)),
               waits=[sq_done] if cc == 0 else (),
               inc=sP if cc == NCH - 1 else None)
        sq_sum_done = (sP, sP.n)
        buf_last["xb"] = sq_sum_done
        mk("vector", lambda e: e.tensor_scalar_mul(m_row[:, :], psS[0][0:1, :], 1.0 / D),
           waits=[sum_done], inc=sV)
        mk("vector", lambda e: e.tensor_scalar_mul(v_row[:, :], psA[4][0:1, :], 1.0 / D),
           waits=[sq_sum_done], inc=sV)
        mk("scalar", lambda e: e.activation(r_row[:, :], m_row[:, :], AF.Square),
           waits=[(sV, sV.n)], inc=sA)
        mk("vector", lambda e: e.scalar_tensor_tensor(v_row[:, :], v_row[:, :], LN_EPS,
                                                      r_row[:, :], ALU.add, ALU.subtract),
           waits=[(sA, sA.n)], inc=sV)
        mk("scalar", lambda e: e.activation(r_row[:, :], v_row[:, :], AF.Sqrt),
           waits=[(sV, sV.n)], inc=sA)
        mk("vector", lambda e: e.reciprocal(r_row[:, :], r_row[:, :]),
           waits=[(sA, sA.n)], inc=sV)
        stats_done = mk("vector",
                        lambda e: e.scalar_tensor_tensor(n_row[:, :], m_row[:, :], -1.0,
                                                         r_row[:, :], ALU.mult, ALU.mult),
                        inc=sV)
        mk("tensor", lambda e: e.matmul(psB[0][:, :], onesf[:, :], r_row[:, :],
                                        start=True, stop=True),
           waits=[stats_done], inc=sP)
        bc_done = mk("tensor", lambda e: e.matmul(psB[1][:, :], onesf[:, :], n_row[:, :],
                                                  start=True, stop=True), inc=sP)
        marks = []
        for cc in range(NCH):
            mk("vector",
               lambda e, cc=cc: e.tensor_mul(ropeA[:, :], chunk(src_f32, cc), psB[0][:, :]),
               waits=[bc_done] if cc == 0 else ())
            marks.append(mk("vector",
                            lambda e, cc=cc: e.tensor_add(chunk(dst_bf16, cc), ropeA[:, :],
                                                          psB[1][:, :]),
                            inc=sV))
        return marks

    def ffn(src_bf16, dst_f32, scale, xln_marks):
        prev_gate_use = [None, None]
        prev_up_use = [None, None]
        for ht in range(NHT):
            par = ht % 2
            slab_g = wsa[:, par * 2048:(par + 1) * 2048]
            slab_u = wsb[:, par * 2048:(par + 1) * 2048]
            dma("sync", slab_g, wg_b.ap()[ht],
                waits=[cast_ready("wg", ht), buf_last[f"wsa{par}"]])
            g_ld = (sL, sL.n)
            dma("scalar", slab_u, wu_b.ap()[ht],
                waits=[cast_ready("wu", ht), buf_last[f"wsb{par}"]], sem=sSt)
            u_ld = (sSt, sSt.n)
            for cc in range(NCH):
                w = []
                if cc == 0:
                    w = [g_ld, prev_gate_use[par]]
                if ht == 0:
                    w.append(xln_marks[cc])
                mk("tensor",
                   lambda e, cc=cc, par=par, slab_g=slab_g: e.matmul(
                       psA[par][:, :], slab_g[:, cc * 128:(cc + 1) * 128],
                       chunk(src_bf16, cc), start=(cc == 0), stop=(cc == NCH - 1)),
                   waits=w, inc=sP if cc == NCH - 1 else None)
            gate_done = (sP, sP.n)
            buf_last[f"wsa{par}"] = gate_done
            for cc in range(NCH):
                mk("tensor",
                   lambda e, cc=cc, par=par, slab_u=slab_u: e.matmul(
                       psA[2 + par][:, :], slab_u[:, cc * 128:(cc + 1) * 128],
                       chunk(src_bf16, cc), start=(cc == 0), stop=(cc == NCH - 1)),
                   waits=[u_ld, prev_up_use[par]] if cc == 0 else (),
                   inc=sP if cc == NCH - 1 else None)
            up_done = (sP, sP.n)
            buf_last[f"wsb{par}"] = up_done
            sil_done = mk("scalar",
                          lambda e, par=par: e.activation(fsil[:, :], psA[par][:, :],
                                                          AF.Silu),
                          waits=[gate_done], inc=sA)
            prev_gate_use[par] = sil_done
            gmul = mk("vector",
                      lambda e, ht=ht, par=par: e.tensor_mul(
                          gated[:, ht * TT:(ht + 1) * TT], fsil[:, :], psA[2 + par][:, :]),
                      waits=[sil_done, up_done], inc=sV)
            prev_up_use[par] = gmul
        gated_done = (sV, sV.n)
        prev_down_use = list(prev_gate_use)
        down_marks = [None, None]
        res_done = None
        for dt in range(NCH):
            par = dt % 2
            dslab = wsd[:, par * 8192:(par + 1) * 8192]
            dma("sync", dslab, wd_b.ap()[dt],
                waits=[cast_ready("wd", dt), buf_last[f"wsd{par}"]])
            dslab_ld = (sL, sL.n)
            dwait = [dslab_ld, gated_done, prev_down_use[par]]
            for hc in range(NHT):
                mk("tensor",
                   lambda e, hc=hc, par=par, dslab=dslab: e.matmul(
                       psA[par][:, :], dslab[:, hc * 128:(hc + 1) * 128],
                       gated[:, hc * TT:(hc + 1) * TT], start=(hc == 0),
                       stop=(hc == NHT - 1)),
                   waits=dwait if hc == 0 else (),
                   inc=sP if hc == NHT - 1 else None)
            down_done = (sP, sP.n)
            down_marks[par] = down_done
            buf_last[f"wsd{par}"] = down_done
            res_done = mk("vector",
                          lambda e, dt=dt, par=par: e.scalar_tensor_tensor(
                              chunk(dst_f32, dt), psA[par][:, :], scale,
                              chunk(dst_f32, dt), ALU.mult, ALU.add),
                          waits=[down_done], inc=sV)
            prev_down_use[par] = res_done
        return res_done

    prev_xa_free = []
    for tt in range(NT):
        ts = slice(tt * TT, (tt + 1) * TT)
        dma("sync", xa.rearrange("p (c t) -> p c t", t=TT),
            x_in[:, ts].rearrange("(c p) t -> p c t", p=128),
            waits=list(prev_xa_free))
        x_ld = (sL, sL.n)
        dma("sync", cost[:, :], cos_in[:, ts], waits=[(sV, sV.n)])
        dma("sync", sint[:, :], sin_in[:, ts])
        trig_ld = (sL, sL.n)

        xln_marks = layernorm(xa, xln, extra_wait=[x_ld])
        y1_done = ffn(xln, xa, 0.5, xln_marks)
        dma("gpsimd", y1d.ap()[:, :, ts].rearrange("c p t -> p c t"),
            xa.rearrange("p (c t) -> p c t", t=TT),
            waits=[y1_done], sem=sSt)
        y1_spill = (sSt, sSt.n)
        y1ln_marks = layernorm(xa, xln, extra_wait=[y1_done])
        y1ln_done = y1ln_marks[-1]
        prev_xa_free = [y1_spill, buf_last["ln_cast"]]

        bank_guard = [None, None, None, None]
        bank = 0

        def qkv_proj(n_out, slab_ld_for, slab_ap_for, evac_for, ready, buf_key_for=None):
            nonlocal bank
            for i in range(n_out):
                slab_ld = slab_ld_for(i)
                bwait = [slab_ld, ready, bank_guard[bank]]
                for cc in range(NCH):
                    mk("tensor",
                       lambda e, i=i, cc=cc, bank=bank, slab_ap_for=slab_ap_for:
                       e.matmul(psA[bank][:, :], slab_ap_for(i, cc), chunk(xln, cc),
                                start=(cc == 0), stop=(cc == NCH - 1)),
                       waits=bwait if cc == 0 else (),
                       inc=sP if cc == NCH - 1 else None)
                mmd = (sP, sP.n)
                if buf_key_for is not None:
                    buf_last[buf_key_for(i)] = mmd
                bank_guard[bank] = evac_for(i, bank, mmd)
                bank = (bank + 1) % 4

        def q_slab_ld(qt):
            par = qt % 2
            dma("sync", wsa[:, par * 2048:(par + 1) * 2048], wq_b.ap()[qt],
                waits=[cast_ready("wq", qt), buf_last[f"wsa{par}"]])
            return (sL, sL.n)

        def q_slab_ap(qt, cc):
            par = qt % 2
            return wsa[:, par * 2048 + cc * 128: par * 2048 + (cc + 1) * 128]

        def q_evac(qt, bk, mmd):
            return mk("scalar",
                      lambda e, qt=qt, bk=bk: e.activation(
                          chunk(qf, qt), psA[bk][:, :], AF.Copy),
                      waits=[mmd, gains_ready], inc=sA)

        qkv_proj(NCH, q_slab_ld, q_slab_ap, q_evac, y1ln_done,
                 buf_key_for=lambda qt: f"wsa{qt % 2}")

        dma("sync", wsd[:, 0:8192].rearrange("p (g t) -> p g t", t=2048),
            wk_b.ap().rearrange("g p t -> p g t"),
            waits=[cast_ready("wk", 0), buf_last["wsd0"]])
        wk_ld = (sL, sL.n)

        def k_slab_ld(kt):
            return wk_ld

        def k_slab_ap(kt, cc):
            return wsd[:, kt * 2048 + cc * 128: kt * 2048 + (cc + 1) * 128]

        def k_evac(kt, bk, mmd):
            return mk("scalar",
                      lambda e, kt=kt, bk=bk: e.activation(
                          chunk(kf, kt), psA[bk][:, :], AF.Copy),
                      waits=[mmd], inc=sA)

        qkv_proj(G, k_slab_ld, k_slab_ap, k_evac, y1ln_done,
                 buf_key_for=lambda kt: "wsd0")

        dma("sync", wsd[:, 8192:8192 + NCH * 512].rearrange("p (c v) -> p c v", v=512),
            wv_b.ap().rearrange("c p v -> p c v"),
            waits=[cast_ready("wv", 0), buf_last["wsd1"]])
        wv_ld = (sL, sL.n)
        for sub in range(4):
            bwait = [wv_ld, y1ln_done, bank_guard[bank]]
            for cc in range(NCH):
                mk("tensor",
                   lambda e, cc=cc, sub=sub, bank=bank: e.matmul(
                       psA[bank][:, :],
                       xln[:, cc * TT + sub * 128: cc * TT + (sub + 1) * 128],
                       wsd[:, 8192 + cc * 512: 8192 + (cc + 1) * 512],
                       start=(cc == 0), stop=(cc == NCH - 1)),
                   waits=bwait if cc == 0 else (),
                   inc=sP if cc == NCH - 1 else None)
            mmd = (sP, sP.n)
            buf_last["wsd1"] = mmd
            bank_guard[bank] = mk("scalar",
                                  lambda e, sub=sub, bank=bank: e.activation(
                                      vtmp[:, sub * 512:(sub + 1) * 512],
                                      psA[bank][:, :], AF.Copy),
                                  waits=[mmd], inc=sA)
            bank = (bank + 1) % 4
        v_done = (sA, sA.n)

        rr_state = {"prev_mm": None, "prev_bc_use": None}

        def rms_rope(src, dst, n_chunks, first_wait, gain):
            prev_mm = rr_state["prev_mm"]
            prev_bc_use = rr_state["prev_bc_use"]
            last = None
            for ci in range(n_chunks):
                sch = chunk(src, ci)
                dch = chunk(dst, ci)
                sqd = mk("scalar",
                         lambda e, sch=sch: e.activation(sqbf[:, :], sch, AF.Square),
                         waits=[first_wait, prev_mm] if ci == 0 else [prev_mm],
                         inc=sA)
                ssum = mk("tensor",
                          lambda e: e.matmul(psS[0][:, :], onesb[:, :], sqbf[:, :],
                                             start=True, stop=True),
                          waits=[sqd], inc=sP)
                prev_mm = ssum
                mk("vector",
                   lambda e: e.tensor_scalar(m_row[:, :], psS[0][0:1, :], 1.0 / DH,
                                             RMS_EPS, ALU.mult, ALU.add),
                   waits=[ssum], inc=sV)
                rms = mk("scalar", lambda e: e.activation(m_row[:, :], m_row[:, :],
                                                          AF.Sqrt),
                         waits=[(sV, sV.n)], inc=sA)
                rcp = mk("vector", lambda e: e.reciprocal(m_row[:, :], m_row[:, :]),
                         waits=[rms], inc=sV)
                bc = mk("tensor", lambda e: e.matmul(psB[0][:, :], onesf[:, :],
                                                     m_row[:, :], start=True, stop=True),
                        waits=[rcp, prev_bc_use], inc=sP)
                qd = mk("vector",
                        lambda e, sch=sch: e.tensor_mul(qn[:, :], sch, psB[0][:, :]),
                        waits=[bc], inc=sV)
                prev_bc_use = qd
                mk("vector",
                   lambda e, ci=ci: e.tensor_scalar_mul(qn[:, :], qn[:, :],
                                                        gain[:, ci:ci + 1]))
                mk("vector", lambda e: e.tensor_copy(ropeA[0:64, :], qn[64:128, :]),
                   waits=[trig_ld])
                mk("vector", lambda e: e.tensor_copy(ropeA[64:128, :], qn[0:64, :]))
                mk("vector", lambda e: e.tensor_mul(ropeB[:, :], qn[:, :], cost[:, :]))
                mk("vector", lambda e: e.tensor_mul(ropeA[:, :], ropeA[:, :], sint[:, :]))
                last = mk("vector",
                          lambda e, dch=dch: e.tensor_add(dch, ropeB[:, :], ropeA[:, :]),
                          inc=sV)
            rr_state["prev_mm"] = prev_mm
            rr_state["prev_bc_use"] = prev_bc_use
            return last

        q_rope_done = rms_rope(qf, qkvb, NCH, (sA, sA.n), gqm)
        k_rope_done = rms_rope(kf, kb, G, (sA, sA.n), gkm)

        qkvb_r = qkvb.rearrange("p (c t) -> p c t", t=TT)
        for d in range(8):
            g2 = d % 4
            dma("gpsimd",
                q_a2a_i.ap()[d].rearrange("(c p) t -> p c t", p=128)[:, :, ts],
                qkvb_r[:, 4 * g2:4 * g2 + 4, :],
                waits=[q_rope_done], sem=sSt)
            dma("gpsimd", k_a2a_i.ap()[d][:, ts], kb[:, g2 * TT:(g2 + 1) * TT],
                waits=[k_rope_done], sem=sSt)
            dma("gpsimd",
                v_a2a_i.ap()[d][tt * TT:(tt + 1) * TT, :]
                .rearrange("(s p) v -> p s v", p=128),
                vtmp.rearrange("p (s v) -> p s v", v=512)[:, :, g2 * 128:(g2 + 1) * 128],
                waits=[v_done], sem=sSt)

    groups8 = [list(range(8))]

    def coll(ins_ap, outs_ap, waits):
        def fn(e, ins_ap=ins_ap, outs_ap=outs_ap,
               waits=tuple(x for x in waits if x is not None)):
            for (s, val) in waits:
                e.wait_ge(s.h, val)
            with allow_ici_all_to_all():
                e.collective_compute("AllToAll", ALU.bypass, replica_groups=groups8,
                                     ins=[ins_ap], outs=[outs_ap]).then_inc(sC.h, 1)
        sC.inc(1)
        plans["gpsimd"].append(fn)
        return (sC, sC.n)

    a2a1_wait = [(sSt, sSt.n)]
    coll(q_a2a_i.ap().rearrange("s q t -> (s q) t"),
         q_a2a_o.ap().rearrange("s q t -> (s q) t"), a2a1_wait)
    coll(k_a2a_i.ap().rearrange("s q t -> (s q) t"),
         k_a2a_o.ap().rearrange("s q t -> (s q) t"), [])
    a2a1_done = coll(v_a2a_i.ap().rearrange("s q t -> (s q) t"),
                     v_a2a_o.ap().rearrange("s q t -> (s q) t"), [])

    def switch_copy(pairs_fn, waits, n_dmas, sem):
        v = sem.inc(16 * n_dmas)

        def fn(e, waits=tuple(x for x in waits if x is not None)):
            for (s, val) in waits:
                e.wait_ge(s.h, val)
            pid = e.partition_id()
            for i in nc.Switch(engines=[e], index=[pid], n=8):
                prs = pairs_fn(i)
                assert len(prs) == n_dmas
                for (o, inp) in prs:
                    e.dma_start(out=o, in_=inp).then_inc(sem.h, 16)
        plans["sync"].append(fn)
        return (sem, v)

    def qkv_pairs(core):
        b = core // 4
        out = []
        for r in range(4):
            out.append((q_mine.ap()[r], q_a2a_o.ap()[4 * b + r]))
            out.append((k_sb[:, r * TOK:(r + 1) * TOK], k_a2a_o.ap()[4 * b + r]))
            out.append((v_sb[:, r * TOK:(r + 1) * TOK].rearrange("p (j d) -> p j d", d=128),
                        v_a2a_o.ap()[4 * b + r].rearrange("(j p) d -> p j d", p=128)))
        return out

    kv_done = switch_copy(qkv_pairs, [a2a1_done], 12, sL)
    qmine_done = kv_done

    qh_marks = {}
    for h in range(2):
        dma("sync", q_sbs[h].rearrange("p (r t) -> p r t", r=4),
            q_mine.ap()[:, h * 128:(h + 1) * 128, :].rearrange("r p t -> p r t"),
            waits=[qmine_done])
        qh_marks[h] = (sL, sL.n)

    P1B = [0, 1, 4]
    p1_guard = [None, None, None]
    p2_guard = [None, None]
    prev_o_evac = None
    prev_sum_cp = None
    prev_obuf_store = [None, None]
    prev_sum_store = [None, None]
    prev_mrow = None
    prev_negM_rd = {}
    negM_marks = {}
    att_tail = []

    def emit_pass2(h, qb):
        nonlocal prev_o_evac, prev_sum_cp
        nj = 4 * qb + 4
        qs = slice(qb * QB, (qb + 1) * QB)
        qsb = q_sbs[h % 2]
        negM_ap = negM[:, qb % 2:qb % 2 + 1]
        pt_rd = {}
        first_exp = None
        last_exp = None
        def emit_st2_exp(jt):
            nonlocal last_exp, first_exp
            par = jt % 2
            diag_r = jt - 4 * qb
            mk("tensor",
               lambda e, jt=jt, par=par, qs=qs, qsb=qsb, diag=(diag_r >= 0):
               e.matmul(psA[2 + par][:, :], k_sb[:, jt * 128:(jt + 1) * 128],
                        qsb[:, qs], start=True, stop=(not diag)),
               waits=[p2_guard[par]],
               inc=None if diag_r >= 0 else sP)
            if diag_r >= 0:
                mk("tensor",
                   lambda e, par=par, r=diag_r: e.matmul(
                       psA[2 + par][:, :], eyeb[:, :],
                       maskp[:, r * 512:(r + 1) * 512], start=False, stop=True),
                   inc=sP)
            st2 = (sP, sP.n)
            ewaits = [st2, pt_rd.get(jt - 4)]
            if jt == 0:
                ewaits.append(negM_marks[(h, qb)])
            ex = mk("scalar",
                    lambda e, jt=jt, par=par, negM_ap=negM_ap: e.activation(
                        pts[jt % 4][:, :], psA[2 + par][:, :], AF.Exp, bias=negM_ap),
                    waits=ewaits, inc=sA)
            p2_guard[par] = ex
            last_exp = ex
            if first_exp is None:
                first_exp = ex
            return ex

        def emit_pv_sum(jt, ex):
            pvw = [ex]
            if jt == 0:
                pvw.append(prev_o_evac)
            mk("tensor",
               lambda e, jt=jt: e.matmul(psB[0][:, :], v_sb[:, jt * 128:(jt + 1) * 128],
                                         pts[jt % 4][:, :], start=(jt == 0),
                                         stop=(jt == nj - 1)),
               waits=pvw, inc=sP if jt == nj - 1 else None)
            sw = []
            if jt == 0:
                sw.append(prev_sum_cp)
            mk("tensor",
               lambda e, jt=jt: e.matmul(psS[0][:, :], onesb[:, :], pts[jt % 4][:, :],
                                         start=(jt == 0), stop=(jt == nj - 1)),
               waits=sw, inc=sP if jt == nj - 1 else None)
            pt_rd[jt] = (sP, sP.n)

        exs = {}
        for jt in range(nj):
            exs[jt] = emit_st2_exp(jt)
            if jt > 0:
                emit_pv_sum(jt - 1, exs[jt - 1])
        emit_pv_sum(nj - 1, exs[nj - 1])
        prev_negM_rd[(h, qb)] = last_exp
        pv_done = (sP, sP.n)
        blk = (h * NQB + qb) % 2
        o_evac = mk("scalar",
                    lambda e, blk=blk: e.activation(obufs[blk][:, :], psB[0][:, :],
                                                    AF.Copy),
                    waits=[pv_done, prev_obuf_store[blk]], inc=sA)
        prev_o_evac = o_evac
        sum_cp = mk("vector",
                    lambda e, blk=blk: e.tensor_copy(sumrows[blk][:, :], psS[0][0:1, :]),
                    waits=[pv_done, prev_sum_store[blk]], inc=sV)
        prev_sum_cp = sum_cp
        quarter = qb // 4
        cslice = slice((qb % 4) * 512, (qb % 4) * 512 + 512)
        for d in (quarter, quarter + 4):
            dma("gpsimd", o_a2a_i.ap()[d][h * 128:(h + 1) * 128, cslice],
                obufs[blk][:, :], waits=[o_evac], sem=sSt)
        prev_obuf_store[blk] = (sSt, sSt.n)
        dma("gpsimd", s_scr.ap()[h][qb:qb + 1, :], sumrows[blk][:, :],
            waits=[sum_cp], sem=sSt)
        prev_sum_store[blk] = (sSt, sSt.n)

    for h in range(HPG):
        q_ld = qh_marks[h]
        qsb = q_sbs[h % 2]
        for qb in range(NQB):
            nj = 4 * qb + 4
            qs = slice(qb * QB, (qb + 1) * QB)
            for jt in range(nj):
                par = jt % 3
                bk = P1B[par]
                diag_r = jt - 4 * qb
                w0 = [p1_guard[par]]
                if jt == 0:
                    w0 += [q_ld, kv_done]
                mk("tensor",
                   lambda e, jt=jt, bk=bk, qs=qs, qsb=qsb, diag=(diag_r >= 0):
                   e.matmul(psA[bk][:, :], k_sb[:, jt * 128:(jt + 1) * 128],
                            qsb[:, qs], start=True, stop=(not diag)),
                   waits=w0, inc=None if diag_r >= 0 else sP)
                if diag_r >= 0:
                    mk("tensor",
                       lambda e, bk=bk, r=diag_r: e.matmul(
                           psA[bk][:, :], eyeb[:, :],
                           maskp[:, r * 512:(r + 1) * 512], start=False, stop=True),
                       inc=sP)
                st_done = (sP, sP.n)
                p1_guard[par] = mk("vector",
                                   lambda e, jt=jt, bk=bk: e.reduce_max(
                                       maxbuf[:, jt:jt + 1], psA[bk][:, :], axis=AX.X),
                                   waits=[st_done], inc=sV)
            if qb > 0:
                emit_pass2(h, qb - 1)
            mcol = mk("vector",
                      lambda e, nj=nj: e.reduce_max(maxcol[:, :], maxbuf[:, 0:nj],
                                                    axis=AX.X),
                      inc=sV)
            tr = mk("tensor",
                    lambda e: e.transpose(psB[1][0:1, 0:128], maxcol[:, :], eyef[:, :]),
                    waits=[mcol, prev_mrow], inc=sP)
            mrow = mk("vector",
                      lambda e: e.reduce_max(srow[0:1, 0:1], psB[1][0:1, 0:128],
                                             axis=AX.X, negate=True),
                      waits=[tr], inc=sV)
            prev_mrow = mrow
            bcM = mk("tensor",
                     lambda e: e.matmul(psB[1][:, 256:257], onesf[:, :], srow[0:1, 0:1],
                                        start=True, stop=True),
                     waits=[mrow, negM_marks.get((h, qb - 1))], inc=sP)
            negM_marks[(h, qb)] = mk("vector",
                                     lambda e, qb=qb: e.tensor_copy(
                                         negM[:, qb % 2:qb % 2 + 1], psB[1][:, 256:257]),
                                     waits=[bcM, prev_negM_rd.get((h, qb - 2))],
                                     inc=sV)
        emit_pass2(h, NQB - 1)
        if h + 2 < HPG:
            dma("sync", q_sbs[h % 2].rearrange("p (r t) -> p r t", r=4),
                q_mine.ap()[:, (h + 2) * 128:(h + 3) * 128, :].rearrange("r p t -> p r t"),
                waits=[(sP, sP.n)])
            qh_marks[h + 2] = (sL, sL.n)

    att_stores = (sSt, sSt.n)
    for d in range(8):
        g2 = d % 4
        dma("gpsimd", s_a2a_i.ap()[d],
            s_scr.ap().rearrange("h q b -> h (q b)")[:, g2 * TOK:(g2 + 1) * TOK],
            waits=[att_stores], sem=sSt)

    coll(o_a2a_i.ap().rearrange("s q t -> (s q) t"),
         o_a2a_o.ap().rearrange("s q t -> (s q) t"), [(sSt, sSt.n)])
    a2a2_done = coll(s_a2a_i.ap().rearrange("s q t -> (s q) t"),
                     s_a2a_o.ap().rearrange("s q t -> (s q) t"), [])

    def o_pairs(core):
        b = core // 4
        out = []
        om = o_mine.ap().rearrange("(g x) p t -> g (x p) t", g=4)
        sm = s_mine.ap().rearrange("(g h) t -> g h t", g=4)
        for g2 in range(4):
            out.append((om[g2], o_a2a_o.ap()[4 * b + g2]))
            out.append((sm[g2], s_a2a_o.ap()[4 * b + g2]))
        return out

    omine_done = switch_copy(o_pairs, [a2a2_done], 8, sL)

    for tt in range(NT):
        ts = slice(tt * TT, (tt + 1) * TT)
        gated_free = [buf_last["wsd0"], buf_last["wsd1"]]
        if tt == 0:
            gated_free = [omine_done, (sP, sP.n), (sV, sV.n), (sA, sA.n)]
        dma("sync", ob.rearrange("p (c t) -> p c t", t=TT),
            o_mine.ap()[:, :, ts].rearrange("c p t -> p c t"),
            waits=gated_free + [omine_done])
        dma("sync", sraw[0:1, :].rearrange("a (c t) -> a c t", c=16),
            s_mine.ap().rearrange("(a c) t -> a c t", a=1)[:, :, ts])
        dma("sync", xa.rearrange("p (c t) -> p c t", t=TT),
            y1d.ap()[:, :, ts].rearrange("c p t -> p c t"),
            waits=(gated_free if tt == 0 else list(prev_xa_free)))
        c_ld = (sL, sL.n)
        rcp = mk("vector", lambda e: e.reciprocal(sraw[:, :], sraw[:, :]),
                 waits=[c_ld], inc=sV)
        prev_norm = [None, None]
        norm_done = None
        for cc in range(NCH):
            par = cc % 2
            bc = mk("tensor",
                    lambda e, cc=cc, par=par: e.matmul(
                        psB[par][:, :], onesf[:, :], sraw[0:1, cc * TT:(cc + 1) * TT],
                        start=True, stop=True),
                    waits=[rcp, prev_norm[par]], inc=sP)
            norm_done = mk("vector",
                           lambda e, cc=cc, par=par: e.tensor_mul(
                               chunk(ob, cc), chunk(ob, cc), psB[par][:, :]),
                           waits=[bc], inc=sV)
            prev_norm[par] = norm_done
        prev_res = [None, None]
        y2_done = None
        for dt in range(NCH):
            par = dt % 2
            dma("sync", wsa[:, par * 2048:(par + 1) * 2048], wo_b.ap()[dt],
                waits=[cast_ready("wo", dt), buf_last[f"wsa{par}"]])
            slab_ld = (sL, sL.n)
            mwait = [slab_ld, norm_done, prev_res[par]]
            for cc in range(NCH):
                mk("tensor",
                   lambda e, dt=dt, cc=cc, par=par: e.matmul(
                       psA[par][:, :],
                       wsa[:, par * 2048 + cc * 128: par * 2048 + (cc + 1) * 128],
                       chunk(ob, cc), start=(cc == 0), stop=(cc == NCH - 1)),
                   waits=mwait if cc == 0 else (),
                   inc=sP if cc == NCH - 1 else None)
            mmd = (sP, sP.n)
            buf_last[f"wsa{par}"] = mmd
            y2_done = mk("vector",
                         lambda e, dt=dt, par=par: e.tensor_add(
                             chunk(xa, dt), psA[par][:, :], chunk(xa, dt)),
                         waits=[mmd], inc=sV)
            prev_res[par] = y2_done
        if y2d is not None:
            dma("gpsimd", y2d.ap()[:, :, ts].rearrange("c p t -> p c t"),
                xa.rearrange("p (c t) -> p c t", t=TT), waits=[y2_done], sem=sSt)
        if obnd is not None:
            dma("gpsimd", obnd.ap()[:, :, ts].rearrange("c p t -> p c t"),
                ob.rearrange("p (c t) -> p c t", t=TT), waits=[norm_done], sem=sSt)
        y2ln_marks = layernorm(xa, xln, extra_wait=[y2_done])
        y3_done = ffn(xln, xa, 0.5, y2ln_marks)
        dma("gpsimd", y_r[:, :, ts].rearrange("c p t -> p c t"),
            xa.rearrange("p (c t) -> p c t", t=TT),
            waits=[y3_done], sem=sO)
        prev_xa_free = [(sO, sO.n), buf_last["ln_cast"]]

    tap_map = {
        "y1": (y1d, [NCH, 128, TOK], F32),
        "y2": (y2d, [NCH, 128, TOK], F32),
        "obn": (obnd, [NCH, 128, TOK], BF16),
        "qin": (q_a2a_i, [8, 512, TOK], BF16),
        "kin": (k_a2a_i, [8, 128, TOK], BF16),
        "vin": (v_a2a_i, [8, TOK, 128], BF16),
        "omine": (o_mine, [NCH, 128, TOK], BF16),
        "smine": (s_mine, [16, TOK], F32),
    }
    for tname in taps:
        src, shape, dt = tap_map[tname]
        tout = nc.dram_tensor("t_" + tname, shape, dt, kind="ExternalOutput").ap()
        dma("gpsimd", tout, src.ap(), waits=[(sO, sO.n), (sSt, sSt.n)], sem=sO)

    fin = (sO, sO.n)

    def fin_fn(e, fin=fin):
        e.wait_ge(fin[0].h, fin[1])
    plans["gpsimd"].append(fin_fn)

    with nc.Block() as block:
        @block.sync
        def _(e):
            for fn in plans["sync"]:
                fn(e)

        @block.tensor
        def _(e):
            for fn in plans["tensor"]:
                fn(e)

        @block.vector
        def _(e):
            for fn in plans["vector"]:
                fn(e)

        @block.scalar
        def _(e):
            for fn in plans["scalar"]:
                fn(e)

        @block.gpsimd
        def _(e):
            for fn in plans["gpsimd"]:
                fn(e)

    stk.close()
    return nc


def _host_inputs(x, wq, wk, wv, wo, gq, gk, w_gate, w_up, w_down):
    bf = ml_dtypes.bfloat16
    wg_sw = np.ascontiguousarray(
        w_gate.reshape(NHT, 128, NCH, 128).transpose(0, 3, 2, 1)).reshape(NHT, 128, 2048)
    wu_sw = np.ascontiguousarray(
        w_up.reshape(NHT, 128, NCH, 128).transpose(0, 3, 2, 1)).reshape(NHT, 128, 2048)
    wd_sw = np.ascontiguousarray(
        w_down.reshape(NCH, 128, NHT, 128).transpose(0, 3, 2, 1)).reshape(NCH, 128, 8192)
    wq_sw = np.ascontiguousarray(
        wq.reshape(NCH, 128, NCH, 128).transpose(2, 1, 0, 3)).reshape(NCH, 128, 2048)
    wk_sw = np.ascontiguousarray(
        wk.reshape(NCH, 128, G, 128).transpose(2, 1, 0, 3)).reshape(G, 128, 2048)
    wv_n = np.ascontiguousarray(wv.reshape(NCH, 128, 512))
    wo_sw = np.ascontiguousarray(
        wo.reshape(NCH, 128, NCH, 128).transpose(2, 1, 0, 3)).reshape(NCH, 128, 2048)
    gq_t = np.ascontiguousarray(gq.reshape(16, 128).T)
    gk_t = np.ascontiguousarray(gk.T)

    dims = np.arange(64, dtype=np.float64)
    ang = np.power(10000.0, -dims / 64.0)
    pos = np.arange(T, dtype=np.float64)
    rad = (pos[None, :] * ang[:, None]).astype(np.float32)
    c64 = np.cos(rad).astype(np.float32)
    s64 = np.sin(rad).astype(np.float32)
    cosw = np.concatenate([c64, c64], axis=0)
    sinw = np.concatenate([-s64, s64], axis=0)

    eyeb = np.eye(128, dtype=bf)
    eyef = np.eye(128, dtype=np.float32)
    onesb = np.ones((128, 1), dtype=bf)
    onesf = np.ones((1, 128), dtype=np.float32)
    maskp = np.zeros((128, 4 * 512), dtype=np.float32)
    jj = np.arange(128)[:, None]
    qq = np.arange(512)[None, :]
    for r in range(4):
        maskp[:, r * 512:(r + 1) * 512] = np.where(qq < jj + 128 * r, NEG_BIG, 0.0)
    maskp = maskp.astype(bf)

    shared = dict(wg=wg_sw, wu=wu_sw, wd=wd_sw, wq=wq_sw, wk=wk_sw, wv=wv_n,
                  wo=wo_sw, gq=gq_t, gk=gk_t, eyeb=eyeb, eyef=eyef,
                  onesb=onesb, onesf=onesf, maskp=maskp)

    in_maps = []
    for c in range(8):
        b, q4 = c // 4, c % 4
        t0 = q4 * TOK
        im = dict(shared)
        im["x"] = np.ascontiguousarray(x[b, t0:t0 + TOK, :].T)
        im["cosw"] = np.ascontiguousarray(cosw[:, t0:t0 + TOK])
        im["sinw"] = np.ascontiguousarray(sinw[:, t0:t0 + TOK])
        in_maps.append(im)
    return in_maps


def kernel(L0_attn_x, L0_attn_wq, L0_attn_wk, L0_attn_wv, L0_attn_wo,
           L0_attn_gq, L0_attn_gk, L0_ffn_w_gate, L0_ffn_w_up, L0_ffn_w_down):
    x = np.asarray(L0_attn_x, dtype=np.float32)
    in_maps = _host_inputs(x,
                           np.asarray(L0_attn_wq, np.float32),
                           np.asarray(L0_attn_wk, np.float32),
                           np.asarray(L0_attn_wv, np.float32),
                           np.asarray(L0_attn_wo, np.float32),
                           np.asarray(L0_attn_gq, np.float32),
                           np.asarray(L0_attn_gk, np.float32),
                           np.asarray(L0_ffn_w_gate, np.float32),
                           np.asarray(L0_ffn_w_up, np.float32),
                           np.asarray(L0_ffn_w_down, np.float32))
    nc = build_nc()
    res = run_bass_kernel_spmd(nc, in_maps, core_ids=list(range(8)))
    out = np.empty((NB, T, D), dtype=np.float32)
    for c in range(8):
        b, q4 = c // 4, c % 4
        t0 = q4 * TOK
        out[b, t0:t0 + TOK, :] = res.results[c]["y"].T
    return out


# version 09 (final)
# speedup vs baseline: 1.2833x; 1.0788x over previous
import contextlib

import numpy as np
import ml_dtypes

import concourse.bass as bass
import concourse.mybir as mybir
from concourse.bass_utils import run_bass_kernel_spmd
from concourse.aot_env import allow_ici_all_to_all

F32 = mybir.dt.float32
BF16 = mybir.dt.bfloat16
AF = mybir.ActivationFunctionType
ALU = mybir.AluOpType
AX = mybir.AxisListType

D = 2048
NB = 2
HID = 8192
T = 8192
TOK = 2048
TT = 512
NT = TOK // TT
NCH = D // 128
NHT = HID // 128
G = 4
HPG = 4
DH = 128
QB = 512
NQB = T // QB
NJT = T // 128
LN_EPS = 1e-5
RMS_EPS = 1e-6
MULT = float(DH) ** -0.25
NEG_BIG = -1e30


class Sem:
    def __init__(self, h):
        self.h = h
        self.n = 0

    def inc(self, k):
        self.n += k
        return self.n


def build_nc(taps=()):
    nc = bass.Bass("TRN2", target_bir_lowering=False, debug=False)

    def din(name, shape, dt=F32):
        return nc.dram_tensor(name, shape, dt, kind="ExternalInput").ap()

    x_in = din("x", [D, TOK])
    wg_in = din("wg", [NHT, 128, 2048])
    wu_in = din("wu", [NHT, 128, 2048])
    wd_in = din("wd", [NCH, 128, 8192])
    wq_in = din("wq", [NCH, 128, 2048])
    wk_in = din("wk", [G, 128, 2048])
    wv_in = din("wv", [NCH, 128, 512])
    wo_in = din("wo", [NCH, 128, 2048])
    gq_in = din("gq", [128, 16])
    gk_in = din("gk", [128, 4])
    cos_in = din("cosw", [128, TOK])
    sin_in = din("sinw", [128, TOK])
    eyeb_in = din("eyeb", [128, 128], BF16)
    eyef_in = din("eyef", [128, 128], F32)
    onesb_in = din("onesb", [128, 1], BF16)
    onesf_in = din("onesf", [1, 128], F32)
    maskp_in = din("maskp", [128, 4 * 512], BF16)

    y_out = nc.dram_tensor("y", [D, TOK], F32, kind="ExternalOutput").ap()

    def dint(name, shape, dt):
        return nc.dram_tensor(name, shape, dt, kind="Internal")

    wg_b = dint("wg_b", [NHT, 128, 2048], BF16)
    wu_b = dint("wu_b", [NHT, 128, 2048], BF16)
    wd_b = dint("wd_b", [NCH, 128, 8192], BF16)
    wq_b = dint("wq_b", [NCH, 128, 2048], BF16)
    wk_b = dint("wk_b", [G, 128, 2048], BF16)
    wv_b = dint("wv_b", [NCH, 128, 512], BF16)
    wo_b = dint("wo_b", [NCH, 128, 2048], BF16)

    y1d = dint("y1d", [NCH, 128, TOK], F32)

    q_a2a_i = dint("q_a2a_i", [8, 512, TOK], BF16)
    q_a2a_o = dint("q_a2a_o", [8, 512, TOK], BF16)
    k_a2a_i = dint("k_a2a_i", [8, 128, TOK], BF16)
    k_a2a_o = dint("k_a2a_o", [8, 128, TOK], BF16)
    v_a2a_i = dint("v_a2a_i", [8, TOK, 128], BF16)
    v_a2a_o = dint("v_a2a_o", [8, TOK, 128], BF16)
    o_a2a_i = dint("o_a2a_i", [8, 512, TOK], BF16)
    o_a2a_o = dint("o_a2a_o", [8, 512, TOK], BF16)
    s_a2a_i = dint("s_a2a_i", [8, 4, TOK], F32)
    s_a2a_o = dint("s_a2a_o", [8, 4, TOK], F32)

    q_mine = dint("q_mine", [4, 512, TOK], BF16)
    o_mine = dint("o_mine", [NCH, 128, TOK], BF16)
    s_mine = dint("s_mine", [16, TOK], F32)
    s_scr = dint("s_scr", [4, NQB, QB], F32)
    y2d = dint("y2d", [NCH, 128, TOK], F32) if "y2" in taps else None
    obnd = dint("obnd", [NCH, 128, TOK], BF16) if "obn" in taps else None

    eyeb = nc.alloc_sbuf_tensor("eyeb_s", [128, 128], BF16)
    eyef = nc.alloc_sbuf_tensor("eyef_s", [128, 128], F32)
    onesb = nc.alloc_sbuf_tensor("onesb_s", [128, 1], BF16)
    onesf = nc.alloc_sbuf_tensor("onesf_s", [1, 128], F32)
    maskp = nc.alloc_sbuf_tensor("maskp_s", [128, 4 * 512], BF16)
    gq_raw = nc.alloc_sbuf_tensor("gq_raw", [128, 16], F32)
    gk_raw = nc.alloc_sbuf_tensor("gk_raw", [128, 4], F32)
    gqm = nc.alloc_sbuf_tensor("gqm_s", [128, 16], F32)
    gkm = nc.alloc_sbuf_tensor("gkm_s", [128, 4], F32)
    cost = nc.alloc_sbuf_tensor("cost", [128, TT], F32)
    sint = nc.alloc_sbuf_tensor("sint", [128, TT], F32)
    m_row = nc.alloc_sbuf_tensor("m_row", [1, TT], F32)
    v_row = nc.alloc_sbuf_tensor("v_row", [1, TT], F32)
    r_row = nc.alloc_sbuf_tensor("r_row", [1, TT], F32)
    n_row = nc.alloc_sbuf_tensor("n_row", [1, TT], F32)
    srow = nc.alloc_sbuf_tensor("srow", [1, 8], F32)
    sumrows = [nc.alloc_sbuf_tensor("sumrow0", [1, QB], F32),
               nc.alloc_sbuf_tensor("sumrow1", [1, QB], F32)]
    negM = nc.alloc_sbuf_tensor("negM", [128, 2], F32)
    maxcol = nc.alloc_sbuf_tensor("maxcol", [128, 1], F32)
    maxbuf = nc.alloc_sbuf_tensor("maxbuf", [128, NJT], F32)
    sqbf = nc.alloc_sbuf_tensor("sqbf", [128, TT], BF16)

    base = (nc.sbuf_base + 31) & ~31

    def sb_at(name, shape, dt, off):
        return nc.alloc_sbuf_tensor_at(name, shape, dt, offset=base + off)

    xa = sb_at("xa", [128, NCH * TT], F32, 0)
    xb = sb_at("xb", [128, NCH * TT], BF16, 32768)
    xln = sb_at("xln", [128, NCH * TT], BF16, 49152)
    gated = sb_at("gated", [128, NHT * TT], BF16, 65536)
    wsa = sb_at("wsa", [128, 2 * 2048], BF16, 131072)
    wsb = sb_at("wsb", [128, 2 * 2048], BF16, 139264)
    wsd = sb_at("wsd", [128, 2 * 8192], BF16, 147456)
    fsil = sb_at("fsil", [128, 512], F32, 180224)
    ropeA = sb_at("ropeA", [128, TT], F32, 182272)
    ropeB = sb_at("ropeB", [128, TT], F32, 184320)
    END_OFF = 186368
    assert base + END_OFF <= nc.sbuf_top, (base, END_OFF, nc.sbuf_top)

    qf = sb_at("qf", [128, NCH * TT], F32, 65536)
    kf = sb_at("kf", [128, G * TT], F32, 65536 + 32768)
    vtmp = sb_at("vtmp", [128, 4 * 512], BF16, 65536 + 40960)
    qkvb = sb_at("qkvb", [128, NCH * TT], BF16, 65536 + 45056)
    kb = sb_at("kb", [128, G * TT], BF16, 65536 + 61440)
    ob = sb_at("ob", [128, NCH * TT], BF16, 65536)
    sraw = sb_at("sraw", [1, 16 * TT], F32, 65536 + 16384)
    qn = sb_at("qn", [128, TT], F32, 180224)

    k_sb = sb_at("k_sb", [128, T], BF16, 0)
    v_sb = sb_at("v_sb", [128, NJT * 128], BF16, 16384)
    q_sbs = [sb_at("q_sb0", [128, T], BF16, 32768),
             sb_at("q_sb1", [128, T], BF16, 49152)]
    pts = [sb_at("pt0", [128, QB], BF16, 65536),
           sb_at("pt1", [128, QB], BF16, 66560),
           sb_at("pt2", [128, QB], BF16, 67584),
           sb_at("pt3", [128, QB], BF16, 68608)]
    obufs = [sb_at("obuf0", [128, QB], BF16, 69632),
             sb_at("obuf1", [128, QB], BF16, 70656)]

    psA = [nc.alloc_psum_tensor(f"psA{i}", [128, 512], F32) for i in range(5)]
    psS = [nc.alloc_psum_tensor("psS0", [1, 512], F32)]
    psB = [nc.alloc_psum_tensor(f"psB{i}", [128, 512], F32) for i in range(2)]

    stk = contextlib.ExitStack()
    sW = Sem(stk.enter_context(nc.semaphore("sW")))
    sL = Sem(stk.enter_context(nc.semaphore("sL")))
    sSt = Sem(stk.enter_context(nc.semaphore("sSt")))
    sP = Sem(stk.enter_context(nc.semaphore("sP")))
    sV = Sem(stk.enter_context(nc.semaphore("sV")))
    sA = Sem(stk.enter_context(nc.semaphore("sA")))
    sC = Sem(stk.enter_context(nc.semaphore("sC")))
    sO = Sem(stk.enter_context(nc.semaphore("sO")))

    plans = {k: [] for k in ["sync", "tensor", "vector", "scalar", "gpsimd"]}

    def mk(eng, build, waits=(), inc=None, amt=1):
        v = inc.inc(amt) if inc is not None else None
        w = tuple(x for x in waits if x is not None)

        def fn(e, build=build, w=w, inc=inc, amt=amt):
            for (s, val) in w:
                e.wait_ge(s.h, val)
            ins = build(e)
            if inc is not None:
                ins.then_inc(inc.h, amt)
        plans[eng].append(fn)
        return (inc, v) if inc is not None else None

    def dma(eng, out, in_, waits=(), sem=None):
        sem = sem if sem is not None else sL
        return mk(eng, lambda e, out=out, in_=in_: e.dma_start(out=out, in_=in_),
                  waits=waits, inc=sem, amt=16)

    def chunk(buf, cc, w=TT):
        return buf[:, cc * w:(cc + 1) * w]

    dma("sync", eyeb[:, :], eyeb_in)
    dma("sync", eyef[:, :], eyef_in)
    dma("sync", onesb[:, :], onesb_in)
    dma("sync", onesf[:, :], onesf_in)
    dma("sync", maskp[:, :], maskp_in)
    dma("sync", gq_raw[:, :], gq_in)
    dma("sync", gk_raw[:, :], gk_in)
    consts_ld = (sL, sL.n)
    mk("vector", lambda e: e.tensor_scalar_mul(gqm[:, :], gq_raw[:, :], MULT),
       waits=[consts_ld], inc=sV)
    mk("vector", lambda e: e.tensor_scalar_mul(gkm[:, :], gk_raw[:, :], MULT),
       waits=[consts_ld], inc=sV)
    gains_ready = (sV, sV.n)

    cast_marks = {}
    N0 = {"wg": NHT, "wu": NHT, "wd": NCH, "wq": NCH, "wk": G, "wv": NCH, "wo": NCH}

    def cast_weight(name, src, dst, nchunks):
        n0 = N0[name]
        step = n0 // nchunks
        marks = []
        for i in range(0, n0, step):
            dma("gpsimd", dst.ap()[i:i + step], src[i:i + step], sem=sW)
            marks.append((sW, sW.n))
        cast_marks[name] = marks

    def cast_interleaved():
        specs = {"wg": (wg_in, wg_b), "wu": (wu_in, wu_b)}
        marks = {"wg": [], "wu": []}
        for i in range(NHT):
            for name in ("wg", "wu"):
                srcT, dst = specs[name]
                dma("gpsimd", dst.ap()[i:i + 1], srcT[i:i + 1], sem=sW)
                marks[name].append((sW, sW.n))
        cast_marks["wg"] = marks["wg"]
        cast_marks["wu"] = marks["wu"]

    cast_interleaved()
    cast_weight("wd", wd_in, wd_b, 16)
    cast_weight("wq", wq_in, wq_b, 2)
    cast_weight("wk", wk_in, wk_b, 1)
    cast_weight("wv", wv_in, wv_b, 1)
    cast_weight("wo", wo_in, wo_b, 2)

    def cast_ready(name, idx):
        marks = cast_marks[name]
        per = N0[name] // len(marks)
        return marks[min(idx // per, len(marks) - 1)]

    x_r = x_in.rearrange("(c p) t -> c p t", p=128)
    y_r = y_out.rearrange("(c p) t -> c p t", p=128)

    buf_last = {"xb": None, "wsa0": None, "wsa1": None, "wsb0": None,
                "wsb1": None, "wsd0": None, "wsd1": None, "ln_cast": None}

    def layernorm(src_f32, dst_bf16, extra_wait=()):
        cast_done = mk("scalar", lambda e: e.activation(xb[:, :], src_f32[:, :], AF.Copy),
                       waits=list(extra_wait) + [buf_last["xb"]], inc=sA)
        buf_last["ln_cast"] = cast_done
        for cc in range(NCH):
            mk("tensor",
               lambda e, cc=cc: e.matmul(psS[0][:, :], onesb[:, :], chunk(xb, cc),
                                         start=(cc == 0), stop=(cc == NCH - 1)),
               waits=[cast_done] if cc == 0 else (),
               inc=sP if cc == NCH - 1 else None)
        sum_done = (sP, sP.n)
        sq_done = mk("scalar", lambda e: e.activation(xb[:, :], xb[:, :], AF.Square),
                     waits=[sum_done], inc=sA)
        for cc in range(NCH):
            mk("tensor",
               lambda e, cc=cc: e.matmul(psA[4][0:1, :], onesb[:, :], chunk(xb, cc),
                                         start=(cc == 0), stop=(cc == NCH - 1)),
               waits=[sq_done] if cc == 0 else (),
               inc=sP if cc == NCH - 1 else None)
        sq_sum_done = (sP, sP.n)
        buf_last["xb"] = sq_sum_done
        mk("vector", lambda e: e.tensor_scalar_mul(m_row[:, :], psS[0][0:1, :], 1.0 / D),
           waits=[sum_done], inc=sV)
        mk("vector", lambda e: e.tensor_scalar_mul(v_row[:, :], psA[4][0:1, :], 1.0 / D),
           waits=[sq_sum_done], inc=sV)
        mk("scalar", lambda e: e.activation(r_row[:, :], m_row[:, :], AF.Square),
           waits=[(sV, sV.n)], inc=sA)
        mk("vector", lambda e: e.scalar_tensor_tensor(v_row[:, :], v_row[:, :], LN_EPS,
                                                      r_row[:, :], ALU.add, ALU.subtract),
           waits=[(sA, sA.n)], inc=sV)
        mk("scalar", lambda e: e.activation(r_row[:, :], v_row[:, :], AF.Sqrt),
           waits=[(sV, sV.n)], inc=sA)
        mk("vector", lambda e: e.reciprocal(r_row[:, :], r_row[:, :]),
           waits=[(sA, sA.n)], inc=sV)
        stats_done = mk("vector",
                        lambda e: e.scalar_tensor_tensor(n_row[:, :], m_row[:, :], -1.0,
                                                         r_row[:, :], ALU.mult, ALU.mult),
                        inc=sV)
        mk("tensor", lambda e: e.matmul(psB[0][:, :], onesf[:, :], r_row[:, :],
                                        start=True, stop=True),
           waits=[stats_done], inc=sP)
        bc_done = mk("tensor", lambda e: e.matmul(psB[1][:, :], onesf[:, :], n_row[:, :],
                                                  start=True, stop=True), inc=sP)
        marks = []
        for cc in range(NCH):
            mk("vector",
               lambda e, cc=cc: e.tensor_mul(ropeA[:, :], chunk(src_f32, cc), psB[0][:, :]),
               waits=[bc_done] if cc == 0 else ())
            marks.append(mk("vector",
                            lambda e, cc=cc: e.tensor_add(chunk(dst_bf16, cc), ropeA[:, :],
                                                          psB[1][:, :]),
                            inc=sV))
        return marks

    def ffn(src_bf16, dst_f32, scale, xln_marks):
        prev_gate_use = [None, None]
        prev_up_use = [None, None]
        for ht in range(NHT):
            par = ht % 2
            slab_g = wsa[:, par * 2048:(par + 1) * 2048]
            slab_u = wsb[:, par * 2048:(par + 1) * 2048]
            dma("sync", slab_g, wg_b.ap()[ht],
                waits=[cast_ready("wg", ht), buf_last[f"wsa{par}"]])
            g_ld = (sL, sL.n)
            dma("scalar", slab_u, wu_b.ap()[ht],
                waits=[cast_ready("wu", ht), buf_last[f"wsb{par}"]], sem=sSt)
            u_ld = (sSt, sSt.n)
            for cc in range(NCH):
                w = []
                if cc == 0:
                    w = [g_ld, prev_gate_use[par]]
                if ht == 0:
                    w.append(xln_marks[cc])
                mk("tensor",
                   lambda e, cc=cc, par=par, slab_g=slab_g: e.matmul(
                       psA[par][:, :], slab_g[:, cc * 128:(cc + 1) * 128],
                       chunk(src_bf16, cc), start=(cc == 0), stop=(cc == NCH - 1)),
                   waits=w, inc=sP if cc == NCH - 1 else None)
            gate_done = (sP, sP.n)
            buf_last[f"wsa{par}"] = gate_done
            for cc in range(NCH):
                mk("tensor",
                   lambda e, cc=cc, par=par, slab_u=slab_u: e.matmul(
                       psA[2 + par][:, :], slab_u[:, cc * 128:(cc + 1) * 128],
                       chunk(src_bf16, cc), start=(cc == 0), stop=(cc == NCH - 1)),
                   waits=[u_ld, prev_up_use[par]] if cc == 0 else (),
                   inc=sP if cc == NCH - 1 else None)
            up_done = (sP, sP.n)
            buf_last[f"wsb{par}"] = up_done
            sil_done = mk("scalar",
                          lambda e, par=par: e.activation(fsil[:, :], psA[par][:, :],
                                                          AF.Silu),
                          waits=[gate_done], inc=sA)
            prev_gate_use[par] = sil_done
            gmul = mk("vector",
                      lambda e, ht=ht, par=par: e.tensor_mul(
                          gated[:, ht * TT:(ht + 1) * TT], fsil[:, :], psA[2 + par][:, :]),
                      waits=[sil_done, up_done], inc=sV)
            prev_up_use[par] = gmul
        gated_done = (sV, sV.n)
        prev_down_use = list(prev_gate_use)
        down_marks = [None, None]
        res_done = None
        for dt in range(NCH):
            par = dt % 2
            dslab = wsd[:, par * 8192:(par + 1) * 8192]
            dma("sync", dslab, wd_b.ap()[dt],
                waits=[cast_ready("wd", dt), buf_last[f"wsd{par}"]])
            dslab_ld = (sL, sL.n)
            dwait = [dslab_ld, gated_done, prev_down_use[par]]
            for hc in range(NHT):
                mk("tensor",
                   lambda e, hc=hc, par=par, dslab=dslab: e.matmul(
                       psA[par][:, :], dslab[:, hc * 128:(hc + 1) * 128],
                       gated[:, hc * TT:(hc + 1) * TT], start=(hc == 0),
                       stop=(hc == NHT - 1)),
                   waits=dwait if hc == 0 else (),
                   inc=sP if hc == NHT - 1 else None)
            down_done = (sP, sP.n)
            down_marks[par] = down_done
            buf_last[f"wsd{par}"] = down_done
            res_done = mk("vector",
                          lambda e, dt=dt, par=par: e.scalar_tensor_tensor(
                              chunk(dst_f32, dt), psA[par][:, :], scale,
                              chunk(dst_f32, dt), ALU.mult, ALU.add),
                          waits=[down_done], inc=sV)
            prev_down_use[par] = res_done
        return res_done

    prev_xa_free = []
    for tt in range(NT):
        ts = slice(tt * TT, (tt + 1) * TT)
        dma("sync", xa.rearrange("p (c t) -> p c t", t=TT),
            x_in[:, ts].rearrange("(c p) t -> p c t", p=128),
            waits=list(prev_xa_free))
        x_ld = (sL, sL.n)
        dma("sync", cost[:, :], cos_in[:, ts], waits=[(sV, sV.n)])
        dma("sync", sint[:, :], sin_in[:, ts])
        trig_ld = (sL, sL.n)

        xln_marks = layernorm(xa, xln, extra_wait=[x_ld])
        y1_done = ffn(xln, xa, 0.5, xln_marks)
        dma("gpsimd", y1d.ap()[:, :, ts].rearrange("c p t -> p c t"),
            xa.rearrange("p (c t) -> p c t", t=TT),
            waits=[y1_done], sem=sSt)
        y1_spill = (sSt, sSt.n)
        y1ln_marks = layernorm(xa, xln, extra_wait=[y1_done])
        y1ln_done = y1ln_marks[-1]
        prev_xa_free = [y1_spill, buf_last["ln_cast"]]

        bank_guard = [None, None, None, None]
        bank = 0

        def qkv_proj(n_out, slab_ld_for, slab_ap_for, evac_for, ready, buf_key_for=None):
            nonlocal bank
            for i in range(n_out):
                slab_ld = slab_ld_for(i)
                bwait = [slab_ld, ready, bank_guard[bank]]
                for cc in range(NCH):
                    mk("tensor",
                       lambda e, i=i, cc=cc, bank=bank, slab_ap_for=slab_ap_for:
                       e.matmul(psA[bank][:, :], slab_ap_for(i, cc), chunk(xln, cc),
                                start=(cc == 0), stop=(cc == NCH - 1)),
                       waits=bwait if cc == 0 else (),
                       inc=sP if cc == NCH - 1 else None)
                mmd = (sP, sP.n)
                if buf_key_for is not None:
                    buf_last[buf_key_for(i)] = mmd
                bank_guard[bank] = evac_for(i, bank, mmd)
                bank = (bank + 1) % 4

        def q_slab_ld(qt):
            par = qt % 2
            dma("sync", wsa[:, par * 2048:(par + 1) * 2048], wq_b.ap()[qt],
                waits=[cast_ready("wq", qt), buf_last[f"wsa{par}"]])
            return (sL, sL.n)

        def q_slab_ap(qt, cc):
            par = qt % 2
            return wsa[:, par * 2048 + cc * 128: par * 2048 + (cc + 1) * 128]

        def q_evac(qt, bk, mmd):
            return mk("scalar",
                      lambda e, qt=qt, bk=bk: e.activation(
                          chunk(qf, qt), psA[bk][:, :], AF.Copy),
                      waits=[mmd, gains_ready], inc=sA)

        qkv_proj(NCH, q_slab_ld, q_slab_ap, q_evac, y1ln_done,
                 buf_key_for=lambda qt: f"wsa{qt % 2}")

        dma("sync", wsd[:, 0:8192].rearrange("p (g t) -> p g t", t=2048),
            wk_b.ap().rearrange("g p t -> p g t"),
            waits=[cast_ready("wk", 0), buf_last["wsd0"]])
        wk_ld = (sL, sL.n)

        def k_slab_ld(kt):
            return wk_ld

        def k_slab_ap(kt, cc):
            return wsd[:, kt * 2048 + cc * 128: kt * 2048 + (cc + 1) * 128]

        def k_evac(kt, bk, mmd):
            return mk("scalar",
                      lambda e, kt=kt, bk=bk: e.activation(
                          chunk(kf, kt), psA[bk][:, :], AF.Copy),
                      waits=[mmd], inc=sA)

        qkv_proj(G, k_slab_ld, k_slab_ap, k_evac, y1ln_done,
                 buf_key_for=lambda kt: "wsd0")

        dma("sync", wsd[:, 8192:8192 + NCH * 512].rearrange("p (c v) -> p c v", v=512),
            wv_b.ap().rearrange("c p v -> p c v"),
            waits=[cast_ready("wv", 0), buf_last["wsd1"]])
        wv_ld = (sL, sL.n)
        for sub in range(4):
            bwait = [wv_ld, y1ln_done, bank_guard[bank]]
            for cc in range(NCH):
                mk("tensor",
                   lambda e, cc=cc, sub=sub, bank=bank: e.matmul(
                       psA[bank][:, :],
                       xln[:, cc * TT + sub * 128: cc * TT + (sub + 1) * 128],
                       wsd[:, 8192 + cc * 512: 8192 + (cc + 1) * 512],
                       start=(cc == 0), stop=(cc == NCH - 1)),
                   waits=bwait if cc == 0 else (),
                   inc=sP if cc == NCH - 1 else None)
            mmd = (sP, sP.n)
            buf_last["wsd1"] = mmd
            bank_guard[bank] = mk("scalar",
                                  lambda e, sub=sub, bank=bank: e.activation(
                                      vtmp[:, sub * 512:(sub + 1) * 512],
                                      psA[bank][:, :], AF.Copy),
                                  waits=[mmd], inc=sA)
            bank = (bank + 1) % 4
        v_done = (sA, sA.n)

        rr_state = {"prev_mm": None, "prev_bc_use": None}

        def rms_rope(src, dst, n_chunks, first_wait, gain):
            prev_mm = rr_state["prev_mm"]
            prev_bc_use = rr_state["prev_bc_use"]
            last = None
            for ci in range(n_chunks):
                sch = chunk(src, ci)
                dch = chunk(dst, ci)
                sqd = mk("scalar",
                         lambda e, sch=sch: e.activation(sqbf[:, :], sch, AF.Square),
                         waits=[first_wait, prev_mm] if ci == 0 else [prev_mm],
                         inc=sA)
                ssum = mk("tensor",
                          lambda e: e.matmul(psS[0][:, :], onesb[:, :], sqbf[:, :],
                                             start=True, stop=True),
                          waits=[sqd], inc=sP)
                prev_mm = ssum
                mk("vector",
                   lambda e: e.tensor_scalar(m_row[:, :], psS[0][0:1, :], 1.0 / DH,
                                             RMS_EPS, ALU.mult, ALU.add),
                   waits=[ssum], inc=sV)
                rms = mk("scalar", lambda e: e.activation(m_row[:, :], m_row[:, :],
                                                          AF.Sqrt),
                         waits=[(sV, sV.n)], inc=sA)
                rcp = mk("vector", lambda e: e.reciprocal(m_row[:, :], m_row[:, :]),
                         waits=[rms], inc=sV)
                bc = mk("tensor", lambda e: e.matmul(psB[0][:, :], onesf[:, :],
                                                     m_row[:, :], start=True, stop=True),
                        waits=[rcp, prev_bc_use], inc=sP)
                qd = mk("vector",
                        lambda e, sch=sch: e.tensor_mul(qn[:, :], sch, psB[0][:, :]),
                        waits=[bc], inc=sV)
                prev_bc_use = qd
                mk("vector",
                   lambda e, ci=ci: e.tensor_scalar_mul(qn[:, :], qn[:, :],
                                                        gain[:, ci:ci + 1]))
                mk("vector", lambda e: e.tensor_copy(ropeA[0:64, :], qn[64:128, :]),
                   waits=[trig_ld])
                mk("vector", lambda e: e.tensor_copy(ropeA[64:128, :], qn[0:64, :]))
                mk("vector", lambda e: e.tensor_mul(ropeB[:, :], qn[:, :], cost[:, :]))
                mk("vector", lambda e: e.tensor_mul(ropeA[:, :], ropeA[:, :], sint[:, :]))
                last = mk("vector",
                          lambda e, dch=dch: e.tensor_add(dch, ropeB[:, :], ropeA[:, :]),
                          inc=sV)
            rr_state["prev_mm"] = prev_mm
            rr_state["prev_bc_use"] = prev_bc_use
            return last

        q_rope_done = rms_rope(qf, qkvb, NCH, (sA, sA.n), gqm)
        k_rope_done = rms_rope(kf, kb, G, (sA, sA.n), gkm)

        qkvb_r = qkvb.rearrange("p (c t) -> p c t", t=TT)
        for d in range(8):
            g2 = d % 4
            dma("gpsimd",
                q_a2a_i.ap()[d].rearrange("(c p) t -> p c t", p=128)[:, :, ts],
                qkvb_r[:, 4 * g2:4 * g2 + 4, :],
                waits=[q_rope_done], sem=sSt)
            dma("gpsimd", k_a2a_i.ap()[d][:, ts], kb[:, g2 * TT:(g2 + 1) * TT],
                waits=[k_rope_done], sem=sSt)
            dma("gpsimd",
                v_a2a_i.ap()[d][tt * TT:(tt + 1) * TT, :]
                .rearrange("(s p) v -> p s v", p=128),
                vtmp.rearrange("p (s v) -> p s v", v=512)[:, :, g2 * 128:(g2 + 1) * 128],
                waits=[v_done], sem=sSt)

    groups8 = [list(range(8))]

    def coll(ins_ap, outs_ap, waits):
        def fn(e, ins_ap=ins_ap, outs_ap=outs_ap,
               waits=tuple(x for x in waits if x is not None)):
            for (s, val) in waits:
                e.wait_ge(s.h, val)
            with allow_ici_all_to_all():
                e.collective_compute("AllToAll", ALU.bypass, replica_groups=groups8,
                                     ins=[ins_ap], outs=[outs_ap]).then_inc(sC.h, 1)
        sC.inc(1)
        plans["gpsimd"].append(fn)
        return (sC, sC.n)

    a2a1_wait = [(sSt, sSt.n)]
    coll(q_a2a_i.ap().rearrange("s q t -> (s q) t"),
         q_a2a_o.ap().rearrange("s q t -> (s q) t"), a2a1_wait)
    coll(k_a2a_i.ap().rearrange("s q t -> (s q) t"),
         k_a2a_o.ap().rearrange("s q t -> (s q) t"), [])
    a2a1_done = coll(v_a2a_i.ap().rearrange("s q t -> (s q) t"),
                     v_a2a_o.ap().rearrange("s q t -> (s q) t"), [])

    def switch_copy(pairs_fn, waits, n_dmas, sem):
        v = sem.inc(16 * n_dmas)

        def fn(e, waits=tuple(x for x in waits if x is not None)):
            for (s, val) in waits:
                e.wait_ge(s.h, val)
            pid = e.partition_id()
            for i in nc.Switch(engines=[e], index=[pid], n=8):
                prs = pairs_fn(i)
                assert len(prs) == n_dmas
                for (o, inp) in prs:
                    e.dma_start(out=o, in_=inp).then_inc(sem.h, 16)
        plans["sync"].append(fn)
        return (sem, v)

    def qkv_pairs(core):
        b = core // 4
        out = []
        for r in range(4):
            out.append((q_mine.ap()[r], q_a2a_o.ap()[4 * b + r]))
            out.append((k_sb[:, r * TOK:(r + 1) * TOK], k_a2a_o.ap()[4 * b + r]))
            out.append((v_sb[:, r * TOK:(r + 1) * TOK].rearrange("p (j d) -> p j d", d=128),
                        v_a2a_o.ap()[4 * b + r].rearrange("(j p) d -> p j d", p=128)))
        return out

    kv_done = switch_copy(qkv_pairs, [a2a1_done], 12, sL)
    qmine_done = kv_done

    qh_marks = {}
    for h in range(2):
        dma("sync", q_sbs[h].rearrange("p (r t) -> p r t", r=4),
            q_mine.ap()[:, h * 128:(h + 1) * 128, :].rearrange("r p t -> p r t"),
            waits=[qmine_done])
        qh_marks[h] = (sL, sL.n)

    P1B = [0, 1, 4]
    p1_guard = [None, None, None]
    p2_guard = [None, None]
    prev_o_evac = None
    prev_sum_cp = None
    prev_obuf_store = [None, None]
    prev_sum_store = [None, None]
    prev_mrow = None
    prev_negM_rd = {}
    negM_marks = {}
    att_tail = []

    def emit_pass2(h, qb):
        nonlocal prev_o_evac, prev_sum_cp
        nj = 4 * qb + 4
        qs = slice(qb * QB, (qb + 1) * QB)
        qsb = q_sbs[h % 2]
        negM_ap = negM[:, qb % 2:qb % 2 + 1]
        pt_rd = {}
        first_exp = None
        last_exp = None
        def emit_st2_exp(jt):
            nonlocal last_exp, first_exp
            par = jt % 2
            diag_r = jt - 4 * qb
            mk("tensor",
               lambda e, jt=jt, par=par, qs=qs, qsb=qsb, diag=(diag_r >= 0):
               e.matmul(psA[2 + par][:, :], k_sb[:, jt * 128:(jt + 1) * 128],
                        qsb[:, qs], start=True, stop=(not diag)),
               waits=[p2_guard[par]],
               inc=None if diag_r >= 0 else sP)
            if diag_r >= 0:
                mk("tensor",
                   lambda e, par=par, r=diag_r: e.matmul(
                       psA[2 + par][:, :], eyeb[:, :],
                       maskp[:, r * 512:(r + 1) * 512], start=False, stop=True),
                   inc=sP)
            st2 = (sP, sP.n)
            ewaits = [st2, pt_rd.get(jt - 4)]
            if jt == 0:
                ewaits.append(negM_marks[(h, qb)])
            ex = mk("scalar",
                    lambda e, jt=jt, par=par, negM_ap=negM_ap: e.activation(
                        pts[jt % 4][:, :], psA[2 + par][:, :], AF.Exp, bias=negM_ap),
                    waits=ewaits, inc=sA)
            p2_guard[par] = ex
            last_exp = ex
            if first_exp is None:
                first_exp = ex
            return ex

        def emit_pv_sum(jt, ex):
            pvw = [ex]
            if jt == 0:
                pvw.append(prev_o_evac)
            mk("tensor",
               lambda e, jt=jt: e.matmul(psB[0][:, :], v_sb[:, jt * 128:(jt + 1) * 128],
                                         pts[jt % 4][:, :], start=(jt == 0),
                                         stop=(jt == nj - 1)),
               waits=pvw, inc=sP if jt == nj - 1 else None)
            sw = []
            if jt == 0:
                sw.append(prev_sum_cp)
            mk("tensor",
               lambda e, jt=jt: e.matmul(psS[0][:, :], onesb[:, :], pts[jt % 4][:, :],
                                         start=(jt == 0), stop=(jt == nj - 1)),
               waits=sw, inc=sP if jt == nj - 1 else None)
            pt_rd[jt] = (sP, sP.n)

        exs = {}
        for jt in range(nj):
            exs[jt] = emit_st2_exp(jt)
            if jt > 0:
                emit_pv_sum(jt - 1, exs[jt - 1])
        emit_pv_sum(nj - 1, exs[nj - 1])
        prev_negM_rd[(h, qb)] = last_exp
        pv_done = (sP, sP.n)
        blk = (h * NQB + qb) % 2
        o_evac = mk("scalar",
                    lambda e, blk=blk: e.activation(obufs[blk][:, :], psB[0][:, :],
                                                    AF.Copy),
                    waits=[pv_done, prev_obuf_store[blk]], inc=sA)
        prev_o_evac = o_evac
        sum_cp = mk("vector",
                    lambda e, blk=blk: e.tensor_copy(sumrows[blk][:, :], psS[0][0:1, :]),
                    waits=[pv_done, prev_sum_store[blk]], inc=sV)
        prev_sum_cp = sum_cp
        quarter = qb // 4
        cslice = slice((qb % 4) * 512, (qb % 4) * 512 + 512)
        for d in (quarter, quarter + 4):
            dma("gpsimd", o_a2a_i.ap()[d][h * 128:(h + 1) * 128, cslice],
                obufs[blk][:, :], waits=[o_evac], sem=sSt)
        prev_obuf_store[blk] = (sSt, sSt.n)
        dma("gpsimd", s_scr.ap()[h][qb:qb + 1, :], sumrows[blk][:, :],
            waits=[sum_cp], sem=sSt)
        prev_sum_store[blk] = (sSt, sSt.n)

    for h in range(HPG):
        q_ld = qh_marks[h]
        qsb = q_sbs[h % 2]
        for qb in range(NQB):
            nj = 4 * qb + 4
            qs = slice(qb * QB, (qb + 1) * QB)
            for jt in range(nj):
                par = jt % 3
                bk = P1B[par]
                diag_r = jt - 4 * qb
                w0 = [p1_guard[par]]
                if jt == 0:
                    w0 += [q_ld, kv_done]
                mk("tensor",
                   lambda e, jt=jt, bk=bk, qs=qs, qsb=qsb, diag=(diag_r >= 0):
                   e.matmul(psA[bk][:, :], k_sb[:, jt * 128:(jt + 1) * 128],
                            qsb[:, qs], start=True, stop=(not diag)),
                   waits=w0, inc=None if diag_r >= 0 else sP)
                if diag_r >= 0:
                    mk("tensor",
                       lambda e, bk=bk, r=diag_r: e.matmul(
                           psA[bk][:, :], eyeb[:, :],
                           maskp[:, r * 512:(r + 1) * 512], start=False, stop=True),
                       inc=sP)
                st_done = (sP, sP.n)
                p1_guard[par] = mk("vector",
                                   lambda e, jt=jt, bk=bk: e.reduce_max(
                                       maxbuf[:, jt:jt + 1], psA[bk][:, :], axis=AX.X),
                                   waits=[st_done], inc=sV)
            if qb > 0:
                emit_pass2(h, qb - 1)
            mcol = mk("vector",
                      lambda e, nj=nj: e.reduce_max(maxcol[:, :], maxbuf[:, 0:nj],
                                                    axis=AX.X),
                      inc=sV)
            tr = mk("tensor",
                    lambda e: e.transpose(psB[1][0:1, 0:128], maxcol[:, :], eyef[:, :]),
                    waits=[mcol, prev_mrow], inc=sP)
            mrow = mk("vector",
                      lambda e: e.reduce_max(srow[0:1, 0:1], psB[1][0:1, 0:128],
                                             axis=AX.X, negate=True),
                      waits=[tr], inc=sV)
            prev_mrow = mrow
            bcM = mk("tensor",
                     lambda e: e.matmul(psB[1][:, 256:257], onesf[:, :], srow[0:1, 0:1],
                                        start=True, stop=True),
                     waits=[mrow, negM_marks.get((h, qb - 1))], inc=sP)
            negM_marks[(h, qb)] = mk("vector",
                                     lambda e, qb=qb: e.tensor_copy(
                                         negM[:, qb % 2:qb % 2 + 1], psB[1][:, 256:257]),
                                     waits=[bcM, prev_negM_rd.get((h, qb - 2))],
                                     inc=sV)
        emit_pass2(h, NQB - 1)
        if h + 2 < HPG:
            dma("sync", q_sbs[h % 2].rearrange("p (r t) -> p r t", r=4),
                q_mine.ap()[:, (h + 2) * 128:(h + 3) * 128, :].rearrange("r p t -> p r t"),
                waits=[(sP, sP.n)])
            qh_marks[h + 2] = (sL, sL.n)

    att_stores = (sSt, sSt.n)
    for d in range(8):
        g2 = d % 4
        dma("gpsimd", s_a2a_i.ap()[d],
            s_scr.ap().rearrange("h q b -> h (q b)")[:, g2 * TOK:(g2 + 1) * TOK],
            waits=[att_stores], sem=sSt)

    coll(o_a2a_i.ap().rearrange("s q t -> (s q) t"),
         o_a2a_o.ap().rearrange("s q t -> (s q) t"), [(sSt, sSt.n)])
    a2a2_done = coll(s_a2a_i.ap().rearrange("s q t -> (s q) t"),
                     s_a2a_o.ap().rearrange("s q t -> (s q) t"), [])

    def o_pairs(core):
        b = core // 4
        out = []
        om = o_mine.ap().rearrange("(g x) p t -> g (x p) t", g=4)
        sm = s_mine.ap().rearrange("(g h) t -> g h t", g=4)
        for g2 in range(4):
            out.append((om[g2], o_a2a_o.ap()[4 * b + g2]))
            out.append((sm[g2], s_a2a_o.ap()[4 * b + g2]))
        return out

    omine_done = switch_copy(o_pairs, [a2a2_done], 8, sL)

    for tt in range(NT):
        ts = slice(tt * TT, (tt + 1) * TT)
        gated_free = [buf_last["wsd0"], buf_last["wsd1"]]
        if tt == 0:
            gated_free = [omine_done, (sP, sP.n), (sV, sV.n), (sA, sA.n)]
        dma("sync", ob.rearrange("p (c t) -> p c t", t=TT),
            o_mine.ap()[:, :, ts].rearrange("c p t -> p c t"),
            waits=gated_free + [omine_done])
        dma("sync", sraw[0:1, :].rearrange("a (c t) -> a c t", c=16),
            s_mine.ap().rearrange("(a c) t -> a c t", a=1)[:, :, ts])
        dma("sync", xa.rearrange("p (c t) -> p c t", t=TT),
            y1d.ap()[:, :, ts].rearrange("c p t -> p c t"),
            waits=(gated_free if tt == 0 else list(prev_xa_free)))
        c_ld = (sL, sL.n)
        rcp = mk("vector", lambda e: e.reciprocal(sraw[:, :], sraw[:, :]),
                 waits=[c_ld], inc=sV)
        prev_norm = [None, None]
        norm_done = None
        for cc in range(NCH):
            par = cc % 2
            bc = mk("tensor",
                    lambda e, cc=cc, par=par: e.matmul(
                        psB[par][:, :], onesf[:, :], sraw[0:1, cc * TT:(cc + 1) * TT],
                        start=True, stop=True),
                    waits=[rcp, prev_norm[par]], inc=sP)
            norm_done = mk("vector",
                           lambda e, cc=cc, par=par: e.tensor_mul(
                               chunk(ob, cc), chunk(ob, cc), psB[par][:, :]),
                           waits=[bc], inc=sV)
            prev_norm[par] = norm_done
        prev_res = [None, None]
        y2_done = None
        for dt in range(NCH):
            par = dt % 2
            dma("sync", wsa[:, par * 2048:(par + 1) * 2048], wo_b.ap()[dt],
                waits=[cast_ready("wo", dt), buf_last[f"wsa{par}"]])
            slab_ld = (sL, sL.n)
            mwait = [slab_ld, norm_done, prev_res[par]]
            for cc in range(NCH):
                mk("tensor",
                   lambda e, dt=dt, cc=cc, par=par: e.matmul(
                       psA[par][:, :],
                       wsa[:, par * 2048 + cc * 128: par * 2048 + (cc + 1) * 128],
                       chunk(ob, cc), start=(cc == 0), stop=(cc == NCH - 1)),
                   waits=mwait if cc == 0 else (),
                   inc=sP if cc == NCH - 1 else None)
            mmd = (sP, sP.n)
            buf_last[f"wsa{par}"] = mmd
            y2_done = mk("vector",
                         lambda e, dt=dt, par=par: e.tensor_add(
                             chunk(xa, dt), psA[par][:, :], chunk(xa, dt)),
                         waits=[mmd], inc=sV)
            prev_res[par] = y2_done
        if y2d is not None:
            dma("gpsimd", y2d.ap()[:, :, ts].rearrange("c p t -> p c t"),
                xa.rearrange("p (c t) -> p c t", t=TT), waits=[y2_done], sem=sSt)
        if obnd is not None:
            dma("gpsimd", obnd.ap()[:, :, ts].rearrange("c p t -> p c t"),
                ob.rearrange("p (c t) -> p c t", t=TT), waits=[norm_done], sem=sSt)
        y2ln_marks = layernorm(xa, xln, extra_wait=[y2_done])
        y3_done = ffn(xln, xa, 0.5, y2ln_marks)
        dma("gpsimd", y_r[:, :, ts].rearrange("c p t -> p c t"),
            xa.rearrange("p (c t) -> p c t", t=TT),
            waits=[y3_done], sem=sO)
        prev_xa_free = [(sO, sO.n), buf_last["ln_cast"]]

    tap_map = {
        "y1": (y1d, [NCH, 128, TOK], F32),
        "y2": (y2d, [NCH, 128, TOK], F32),
        "obn": (obnd, [NCH, 128, TOK], BF16),
        "qin": (q_a2a_i, [8, 512, TOK], BF16),
        "kin": (k_a2a_i, [8, 128, TOK], BF16),
        "vin": (v_a2a_i, [8, TOK, 128], BF16),
        "omine": (o_mine, [NCH, 128, TOK], BF16),
        "smine": (s_mine, [16, TOK], F32),
    }
    for tname in taps:
        src, shape, dt = tap_map[tname]
        tout = nc.dram_tensor("t_" + tname, shape, dt, kind="ExternalOutput").ap()
        dma("gpsimd", tout, src.ap(), waits=[(sO, sO.n), (sSt, sSt.n)], sem=sO)

    fin = (sO, sO.n)

    def fin_fn(e, fin=fin):
        e.wait_ge(fin[0].h, fin[1])
    plans["gpsimd"].append(fin_fn)

    with nc.Block() as block:
        @block.sync
        def _(e):
            for fn in plans["sync"]:
                fn(e)

        @block.tensor
        def _(e):
            for fn in plans["tensor"]:
                fn(e)

        @block.vector
        def _(e):
            for fn in plans["vector"]:
                fn(e)

        @block.scalar
        def _(e):
            for fn in plans["scalar"]:
                fn(e)

        @block.gpsimd
        def _(e):
            for fn in plans["gpsimd"]:
                fn(e)

    stk.close()
    return nc


def _host_inputs(x, wq, wk, wv, wo, gq, gk, w_gate, w_up, w_down):
    bf = ml_dtypes.bfloat16
    wg_sw = np.ascontiguousarray(
        w_gate.reshape(NHT, 128, NCH, 128).transpose(0, 3, 2, 1)).reshape(NHT, 128, 2048)
    wu_sw = np.ascontiguousarray(
        w_up.reshape(NHT, 128, NCH, 128).transpose(0, 3, 2, 1)).reshape(NHT, 128, 2048)
    wd_sw = np.ascontiguousarray(
        w_down.reshape(NCH, 128, NHT, 128).transpose(0, 3, 2, 1)).reshape(NCH, 128, 8192)
    wq_sw = np.ascontiguousarray(
        wq.reshape(NCH, 128, NCH, 128).transpose(2, 1, 0, 3)).reshape(NCH, 128, 2048)
    wk_sw = np.ascontiguousarray(
        wk.reshape(NCH, 128, G, 128).transpose(2, 1, 0, 3)).reshape(G, 128, 2048)
    wv_n = np.ascontiguousarray(wv.reshape(NCH, 128, 512))
    wo_sw = np.ascontiguousarray(
        wo.reshape(NCH, 128, NCH, 128).transpose(2, 1, 0, 3)).reshape(NCH, 128, 2048)
    gq_t = np.ascontiguousarray(gq.reshape(16, 128).T)
    gk_t = np.ascontiguousarray(gk.T)

    dims = np.arange(64, dtype=np.float64)
    ang = np.power(10000.0, -dims / 64.0)
    pos = np.arange(T, dtype=np.float64)
    rad = (pos[None, :] * ang[:, None]).astype(np.float32)
    c64 = np.cos(rad).astype(np.float32)
    s64 = np.sin(rad).astype(np.float32)
    cosw = np.concatenate([c64, c64], axis=0)
    sinw = np.concatenate([-s64, s64], axis=0)

    eyeb = np.eye(128, dtype=bf)
    eyef = np.eye(128, dtype=np.float32)
    onesb = np.ones((128, 1), dtype=bf)
    onesf = np.ones((1, 128), dtype=np.float32)
    maskp = np.zeros((128, 4 * 512), dtype=np.float32)
    jj = np.arange(128)[:, None]
    qq = np.arange(512)[None, :]
    for r in range(4):
        maskp[:, r * 512:(r + 1) * 512] = np.where(qq < jj + 128 * r, NEG_BIG, 0.0)
    maskp = maskp.astype(bf)

    shared = dict(wg=wg_sw, wu=wu_sw, wd=wd_sw, wq=wq_sw, wk=wk_sw, wv=wv_n,
                  wo=wo_sw, gq=gq_t, gk=gk_t, eyeb=eyeb, eyef=eyef,
                  onesb=onesb, onesf=onesf, maskp=maskp)

    in_maps = []
    for c in range(8):
        b, q4 = c // 4, c % 4
        t0 = q4 * TOK
        im = dict(shared)
        im["x"] = np.ascontiguousarray(x[b, t0:t0 + TOK, :].T)
        im["cosw"] = np.ascontiguousarray(cosw[:, t0:t0 + TOK])
        im["sinw"] = np.ascontiguousarray(sinw[:, t0:t0 + TOK])
        in_maps.append(im)
    return in_maps


def kernel(L0_attn_x, L0_attn_wq, L0_attn_wk, L0_attn_wv, L0_attn_wo,
           L0_attn_gq, L0_attn_gk, L0_ffn_w_gate, L0_ffn_w_up, L0_ffn_w_down):
    x = np.asarray(L0_attn_x, dtype=np.float32)
    in_maps = _host_inputs(x,
                           np.asarray(L0_attn_wq, np.float32),
                           np.asarray(L0_attn_wk, np.float32),
                           np.asarray(L0_attn_wv, np.float32),
                           np.asarray(L0_attn_wo, np.float32),
                           np.asarray(L0_attn_gq, np.float32),
                           np.asarray(L0_attn_gk, np.float32),
                           np.asarray(L0_ffn_w_gate, np.float32),
                           np.asarray(L0_ffn_w_up, np.float32),
                           np.asarray(L0_ffn_w_down, np.float32))
    nc = build_nc()
    res = run_bass_kernel_spmd(nc, in_maps, core_ids=list(range(8)))
    out = np.empty((NB, T, D), dtype=np.float32)
    for c in range(8):
        b, q4 = c // 4, c % 4
        t0 = q4 * TOK
        out[b, t0:t0 + TOK, :] = res.results[c]["y"].T
    return out
